# version 17
# baseline (speedup 1.0000x reference)
import math
import numpy as np
import ml_dtypes
import concourse.bass as bass
import concourse.mybir as mybir
from concourse.bass_utils import run_bass_kernel_spmd
from contextlib import ExitStack

F32 = mybir.dt.float32
BF16 = mybir.dt.bfloat16
AF = mybir.ActivationFunctionType
ALU = mybir.AluOpType
AX = mybir.AxisListType

D = 2048
DQ = 4096
NH = 32
NKV = 8
HD = 128
IF = 8192
TOK = 1024
NKC = D // 128
EPS = 1e-5
EXP_BIAS = -12.0
SCALE = 1.0 / math.sqrt(HD)


class Ctr:
    def __init__(self, h):
        self.h = h
        self.n = 0

    def inc(self, ins, k=1):
        ins.then_inc(self.h, k)
        self.n += k
        return self.n


_DEBUG = [False]


def build_program():
    nc = bass.Bass("TRN2", target_bir_lowering=False)

    u_c = nc.dram_tensor("u_c", [TOK, D], F32, kind="ExternalInput").ap()
    wi_blk = nc.dram_tensor("wi_blk", [48, 128, NKC, 128], F32, kind="ExternalInput").ap()
    wg_blk = nc.dram_tensor("wg_blk", [64, 128, NKC, 128], F32, kind="ExternalInput").ap()
    wu_blk = nc.dram_tensor("wu_blk", [64, 128, NKC, 128], F32, kind="ExternalInput").ap()
    wd_blk = nc.dram_tensor("wd_blk", [16, 128, 64, 128], F32, kind="ExternalInput").ap()
    wo_blk = nc.dram_tensor("wo_blk", [16, 128, 32, 128], F32, kind="ExternalInput").ap()
    cos_t = nc.dram_tensor("cos_t", [128, TOK], F32, kind="ExternalInput").ap()
    sin_t = nc.dram_tensor("sin_t", [128, TOK], F32, kind="ExternalInput").ap()
    hostm = nc.dram_tensor("hostm", [128, 4, 128], BF16, kind="ExternalInput").ap()
    rott = nc.dram_tensor("rott", [128, 128], F32, kind="ExternalInput").ap()
    idf = nc.dram_tensor("idf", [128, 128], F32, kind="ExternalInput").ap()
    onesc = nc.dram_tensor("onesc", [128, 1], BF16, kind="ExternalInput").ap()
    ones1 = nc.dram_tensor("ones1", [1, 128], BF16, kind="ExternalInput").ap()
    out_d = nc.dram_tensor("out", [TOK, D], F32, kind="ExternalOutput").ap()

    k_loc = nc.dram_tensor("k_loc", [NKV * HD, TOK], BF16).ap()
    v_loc = nc.dram_tensor("v_loc", [NKV * TOK, HD], BF16).ap()
    k_all = nc.dram_tensor("k_all", [4 * NKV * HD, TOK], BF16).ap()
    v_all = nc.dram_tensor("v_all", [4 * NKV * TOK, HD], BF16).ap()
    q_sp = nc.dram_tensor("q_sp", [DQ, TOK], BF16).ap()
    g_sp = nc.dram_tensor("g_sp", [D, TOK], BF16).ap()
    if _DEBUG[0]:
        dbg_xnT = nc.dram_tensor("dbg_xnT", [128, NKC, TOK], BF16, kind="ExternalOutput").ap()
        dbg_k = nc.dram_tensor("dbg_k", [NKV * HD, TOK], BF16, kind="ExternalOutput").ap()
        dbg_v = nc.dram_tensor("dbg_v", [NKV * TOK, HD], BF16, kind="ExternalOutput").ap()
        dbg_q = nc.dram_tensor("dbg_q", [DQ, TOK], BF16, kind="ExternalOutput").ap()
        dbg_g = nc.dram_tensor("dbg_g", [D, TOK], BF16, kind="ExternalOutput").ap()
        dbg_ao = nc.dram_tensor("dbg_ao", [128, NH, TOK], BF16, kind="ExternalOutput").ap()
        dbg_hs = nc.dram_tensor("dbg_hs", [128, 64, 512], BF16, kind="ExternalOutput").ap()
        dbg_sg = nc.dram_tensor("dbg_sg", [128, 16, TOK], F32, kind="ExternalOutput").ap()

    top = ExitStack()
    with top:
        mksem = lambda name: Ctr(top.enter_context(nc.semaphore(name)))
        sCN = mksem("sCN")
        su = [mksem(f"su{i}") for i in range(2)]
        skv = [mksem(f"skv{i}") for i in range(2)]
        sqb = [mksem(f"sqb{i}") for i in range(3)]
        sgb = [mksem(f"sgb{i}") for i in range(2)]
        sw = [mksem(f"sw{i}") for i in range(6)]
        svst = [mksem(f"svst{i}") for i in range(2)]
        sks = mksem("sks")
        sgst = [mksem(f"sgst{i}") for i in range(2)]
        sqst = [mksem(f"sqst{i}") for i in range(2)]
        sost = [mksem(f"sost{i}") for i in range(2)]
        sdbg = mksem("sdbg")
        sPE = mksem("sPE")
        sPEb = mksem("sPEb")
        sVE = mksem("sVE")
        sSE = mksem("sSE")
        sCC = mksem("sCC")
        sA = mksem("sA")
        sB = mksem("sB")
        sEX = mksem("sEX")
        sMK = mksem("sMK")

        block = top.enter_context(nc.Block())

        res = {}

        def wait_free(eng, name):
            if name in res:
                ctr, val = res[name]
                eng.wait_ge(ctr.h, val)

        def set_free(name, ctr, val):
            res[name] = (ctr, val)

        cos_sb = top.enter_context(nc.sbuf_tensor("cos_sb", [128, TOK], F32))
        sin_sb = top.enter_context(nc.sbuf_tensor("sin_sb", [128, TOK], F32))
        hostm_sb = top.enter_context(nc.sbuf_tensor("hostm_sb", [128, 4, 128], BF16))
        rott_sb = top.enter_context(nc.sbuf_tensor("rott_sb", [128, 128], F32))
        idf_sb = top.enter_context(nc.sbuf_tensor("idf_sb", [128, 128], F32))
        onesc_sb = top.enter_context(nc.sbuf_tensor("onesc_sb", [128, 1], BF16))
        ones1_sb = top.enter_context(nc.sbuf_tensor("ones1_sb", [1, 128], BF16))
        ebias_sb = top.enter_context(nc.sbuf_tensor("ebias_sb", [128, 1], F32))
        eps_sb = top.enter_context(nc.sbuf_tensor("eps_sb", [128, 1], F32))
        wpool = top.enter_context(nc.sbuf_tensor("wpool", [128, 6, 4096], BF16))

        def wslot(i, nk):
            return wpool[:, i, :].rearrange("p (k c) -> p k c", c=128)[:, 0:nk, :]

        @block.sync
        def _(sy):
            for dst, src in [(cos_sb[:, :], cos_t), (sin_sb[:, :], sin_t),
                             (hostm_sb[:, :, :], hostm), (rott_sb[:, :], rott),
                             (idf_sb[:, :], idf), (onesc_sb[:, :], onesc),
                             (ones1_sb[:, :], ones1)]:
                sCN.inc(sy.dma_start(out=dst, in_=src), 16)
        CONST_LD = sCN.n

        @block.vector
        def _(v):
            v.memset(ebias_sb[:, :], EXP_BIAS)
            sVE.inc(v.memset(eps_sb[:, :], EPS))
        CONST_VE = sVE.n

        @block.tensor
        def _(t):
            t.wait_ge(sCN.h, CONST_LD)

        @block.vector
        def _(v):
            v.wait_ge(sCN.h, CONST_LD)

        @block.scalar
        def _(s):
            s.wait_ge(sCN.h, CONST_LD)

        s_xnT = ExitStack()
        xnT = s_xnT.enter_context(nc.sbuf_tensor("xnT", [128, NKC, TOK], BF16))
        s_p14 = ExitStack()
        pa = [s_p14.enter_context(nc.psum_tensor(f"pa{i}", [128, 512], F32)) for i in range(2)]
        pp = [s_p14.enter_context(nc.psum_tensor(f"pp{i}", [128, 512], F32)) for i in range(2)]

        s_p1 = ExitStack()
        u_buf = [s_p1.enter_context(nc.sbuf_tensor(f"u{i}", [128, D], F32)) for i in range(2)]
        sq_buf = [s_p1.enter_context(nc.sbuf_tensor("sq0", [128, D], F32))]
        xf_buf = [s_p1.enter_context(nc.sbuf_tensor(f"xf{i}", [128, D], F32)) for i in range(2)]
        st_buf = s_p1.enter_context(nc.sbuf_tensor("st", [128, 18], F32))

        uload = [0] * 8
        xnt_tt_mark = [0] * 8

        @block.sync
        def _(sy):
            for t in range(2):
                uload[t] = su[t].inc(sy.dma_start(out=u_buf[t][:, :], in_=u_c[t * 128:(t + 1) * 128, :]), 16)

        SE1 = sSE.n
        xf_free_mark = [0] * 8
        e2_mark = [0] * 8
        var_mark = [0] * 8
        bias_mark = [0] * 8
        mean_mark = [0] * 8

        for tt in range(8):
            par = tt % 2
            B = par * 9

            @block.vector
            def _(v, tt=tt, par=par, B=B):
                v.wait_ge(su[par].h, uload[tt])
                sVE.inc(v.reduce_sum(st_buf[:, B + 0:B + 1], u_buf[par][:, :], axis=AX.X))
                v.wait_ge(sVE.h, sVE.n)
                mean_mark[tt] = sVE.inc(v.tensor_scalar_mul(st_buf[:, B + 1:B + 2],
                                                            st_buf[:, B + 0:B + 1], 1.0 / D))
                v.wait_ge(sSE.h, SE1 + 3 * tt + 1)
                sVE.inc(v.tensor_scalar_mul(st_buf[:, B + 3:B + 4], st_buf[:, B + 2:B + 3], 1.0 / D))
                e2_mark[tt] = sVE.n
                sVE.inc(v.tensor_tensor(st_buf[:, B + 4:B + 5], st_buf[:, B + 1:B + 2],
                                        st_buf[:, B + 1:B + 2], ALU.mult))
                v.wait_ge(sVE.h, sVE.n)
                var_mark[tt] = sVE.inc(v.tensor_tensor(st_buf[:, B + 5:B + 6], st_buf[:, B + 3:B + 4],
                                                       st_buf[:, B + 4:B + 5], ALU.subtract))
                v.wait_ge(sSE.h, SE1 + 3 * tt + 2)
                sVE.inc(v.reciprocal(st_buf[:, B + 7:B + 8], st_buf[:, B + 6:B + 7]))
                v.wait_ge(sVE.h, sVE.n)
                bias_mark[tt] = sVE.inc(v.scalar_tensor_tensor(
                    st_buf[:, B + 8:B + 9], st_buf[:, B + 1:B + 2], -1.0,
                    st_buf[:, B + 7:B + 8], ALU.mult, ALU.mult))

            @block.sync
            def _(sy, tt=tt, par=par):
                if tt + 2 < 8:
                    sy.wait_ge(sVE.h, mean_mark[tt])
                    sy.wait_ge(sSE.h, SE1 + 3 * tt + 3)
                    uload[tt + 2] = su[par].inc(
                        sy.dma_start(out=u_buf[par][:, :], in_=u_c[(tt + 2) * 128:(tt + 3) * 128, :]), 16)

            for grp in range(4):
                gpar = grp % 2

                @block.tensor
                def _(t, tt=tt, grp=grp, gpar=gpar):
                    t.wait_ge(sSE.h, SE1 + 3 * tt + 3)
                    wait_free(t, f"pp{gpar}")
                    ins = None
                    for j in range(4):
                        kc = grp * 4 + j
                        ins = t.transpose(pp[gpar][:, j * 128:(j + 1) * 128],
                                          xf_buf[tt % 2][:, kc * 128:(kc + 1) * 128], idf_sb[:, :])
                    sPE.inc(ins)
                tr_done = sPE.n

                @block.vector
                def _(v, tt=tt, grp=grp, gpar=gpar, tr_done=tr_done):
                    v.wait_ge(sPE.h, tr_done)
                    m = sVE.inc(v.tensor_copy(
                        xnT[:, grp * 4:(grp + 1) * 4, tt * 128:(tt + 1) * 128],
                        pp[gpar][:, :].rearrange("p (j c) -> p j c", c=128)))
                    set_free(f"pp{gpar}", sVE, m)
                    if grp == 3:
                        xf_free_mark[tt] = m
            xnt_tt_mark[tt] = sVE.n

        @block.scalar
        def _(s):
            s.wait_ge(sVE.h, CONST_VE)
            for tt in range(8):
                par = tt % 2
                B = par * 9
                s.wait_ge(su[par].h, uload[tt])
                if tt >= 2:
                    s.wait_ge(sVE.h, e2_mark[tt - 2])
                sSE.inc(s.activation(sq_buf[0][:, :], u_buf[par][:, :], AF.Square,
                                     accum_out=st_buf[:, B + 2:B + 3]))
                s.wait_ge(sVE.h, var_mark[tt])
                sSE.inc(s.activation(st_buf[:, B + 6:B + 7], st_buf[:, B + 5:B + 6], AF.Sqrt,
                                     bias=eps_sb[:, 0:1], scale=1.0))
                s.wait_ge(sVE.h, bias_mark[tt])
                if tt >= 2:
                    s.wait_ge(sVE.h, xf_free_mark[tt - 2])
                sSE.inc(s.activation(xf_buf[par][:, :], u_buf[par][:, :], AF.Identity,
                                     bias=st_buf[:, B + 8:B + 9], scale=st_buf[:, B + 7:B + 8]))
            assert sSE.n == SE1 + 24

        XNT_DONE = ("VE", sVE.n)
        if _DEBUG[0]:
            @block.sync
            def _(sy):
                sy.wait_ge(sVE.h, XNT_DONE[1])
                sdbg.inc(sy.dma_start(out=dbg_xnT, in_=xnT[:, :, :]), 16)
        s_p1.close()

        s_p2 = ExitStack()
        k_stage = s_p2.enter_context(nc.sbuf_tensor("k_stage", [128, NKV, TOK], BF16))
        tmp2 = [s_p2.enter_context(nc.sbuf_tensor(f"tmp2_{i}", [128, 512], F32)) for i in range(2)]
        rp1_2 = s_p2.enter_context(nc.sbuf_tensor("rp1_2", [128, 512], F32))
        rp2_2 = s_p2.enter_context(nc.sbuf_tensor("rp2_2", [128, 512], F32))
        v_stage = [s_p2.enter_context(nc.sbuf_tensor(f"vst{i}", [128, 512], BF16)) for i in range(2)]

        wld = {}

        def wload(gp, blk, bidx, slot, key):
            wait_free(gp, f"wslot{slot}")
            wld[key] = sw[slot].inc(gp.dma_start(out=wslot(slot, NKC), in_=blk[bidx, :, :, :]), 16)

        def matmul_accum(t, psum_ap, slot, rhs_fn, wmark):
            t.wait_ge(sw[slot].h, wmark)
            ins = None
            for kc in range(NKC):
                ins = t.matmul(psum_ap, wslot(slot, NKC)[:, kc, :], rhs_fn(kc),
                               start=(kc == 0), stop=(kc == NKC - 1))
            return ins

        kst_last = [0]

        def p2_stage2(t, u):
            par = u["par"]
            t.wait_ge(sVE.h, u["tmp_done"])
            wait_free(t, f"pp{par}")
            if u["is_v"]:
                ins = None
                for j in range(4):
                    ins = t.transpose(pp[par][:, j * 128:(j + 1) * 128],
                                      tmp2[par][:, j * 128:(j + 1) * 128], idf_sb[:, :])
            else:
                ins = t.matmul(pp[par][:, :], rott_sb[:, :], tmp2[par][:, :],
                               start=True, stop=True)
            u["pe2_done"] = sPE.inc(ins)

        def p2_post(u):
            par, tt2, kvh = u["par"], u["tt2"], u["kvh"]
            if u["is_v"]:
                @block.vector
                def _(v):
                    v.wait_ge(sPE.h, u["pe2_done"])
                    wait_free(v, f"vst{par}")
                    m = sVE.inc(v.tensor_copy(v_stage[par][:, :], pp[par][:, :]))
                    set_free(f"pp{par}", sVE, m)
                    set_free(f"tmp{par}", sVE, m)
                vev_done = sVE.n

                @block.sync
                def _(sy):
                    sy.wait_ge(sVE.h, vev_done)
                    base = kvh * TOK + tt2 * 512
                    dst = v_loc[base:base + 512, :].rearrange("(jt j) d -> j jt d", j=128)
                    m = svst[par].inc(sy.dma_start(
                        out=dst, in_=v_stage[par][:, :].rearrange("j (jt d) -> j jt d", d=128)), 16)
                    set_free(f"vst{par}", svst[par], m)
            else:
                @block.vector
                def _(v):
                    cs = cos_sb[:, tt2 * 512:(tt2 + 1) * 512]
                    sn = sin_sb[:, tt2 * 512:(tt2 + 1) * 512]
                    v.tensor_tensor(rp1_2[:, :], tmp2[par][:, :], cs, ALU.mult)
                    v.wait_ge(sPE.h, u["pe2_done"])
                    v.tensor_tensor(rp2_2[:, :], pp[par][:, :], sn, ALU.mult)
                    m = sVE.inc(v.tensor_tensor(k_stage[:, kvh, tt2 * 512:(tt2 + 1) * 512],
                                                rp1_2[:, :], rp2_2[:, :], ALU.add))
                    set_free(f"tmp{par}", sVE, m)
                    set_free(f"pp{par}", sVE, m)
                    kst_last[0] = m

        pending = [None]
        for oc in range(16):
            slot = oc % 4

            @block.gpsimd
            def _(gp, oc=oc):
                if oc == 0:
                    for j in range(3):
                        wload(gp, wi_blk, j, j, ("wi", j))
                if oc == 1:
                    for un, sl in [(0, 4), (1, 5)]:
                        cc = (un % 4) * 16 + un // 4
                        sw[sl].inc(gp.dma_start(
                            out=wpool[:, sl, 0:2048].rearrange("p (k c) -> p k c", c=128),
                            in_=wg_blk[cc, :, :, :]), 16)
                        wld[("gu", 0, un)] = sw[sl].inc(gp.dma_start(
                            out=wpool[:, sl, 2048:4096].rearrange("p (k c) -> p k c", c=128),
                            in_=wu_blk[cc, :, :, :]), 16)
                if oc + 3 < 16:
                    wload(gp, wi_blk, oc + 3, (oc + 3) % 4, ("wi", oc + 3))

            for tt2 in range(2):
                n2 = oc * 2 + tt2
                par = n2 % 2
                u = dict(par=par, tt2=tt2, kvh=oc if oc < 8 else oc - 8, is_v=(oc < 8))

                @block.tensor
                def _(t, oc=oc, tt2=tt2, par=par, slot=slot):
                    if oc == 0 and tt2 == 0:
                        t.wait_ge(sVE.h, XNT_DONE[1])
                    wait_free(t, f"pa{par}")
                    ins = matmul_accum(t, pa[par][:, :], slot,
                                       lambda kc: xnT[:, kc, tt2 * 512:(tt2 + 1) * 512],
                                       wld[("wi", oc)])
                    m = sPE.inc(ins)
                    if tt2 == 1:
                        set_free(f"wslot{slot}", sPE, m)
                    if pending[0] is not None:
                        p2_stage2(t, pending[0])
                u["acc_done"] = sPE.n

                @block.vector
                def _(v, par=par, u=u):
                    v.wait_ge(sPE.h, u["acc_done"])
                    wait_free(v, f"tmp{par}")
                    m = sVE.inc(v.tensor_copy(tmp2[par][:, :], pa[par][:, :]))
                    set_free(f"pa{par}", sVE, m)
                u["tmp_done"] = sVE.n

                if pending[0] is not None:
                    p2_post(pending[0])
                pending[0] = u

        @block.tensor
        def _(t):
            p2_stage2(t, pending[0])
        p2_post(pending[0])
        pending[0] = None

        @block.sync
        def _(sy):
            sy.wait_ge(sVE.h, kst_last[0])
            sks.inc(sy.dma_start(out=k_loc.rearrange("(h d) t -> d h t", d=128),
                                 in_=k_stage[:, :, :]), 16)

        def emit_collectives():
            @block.gpsimd
            def _(gp):
                gp.wait_ge(svst[0].h, svst[0].n)
                gp.wait_ge(svst[1].h, svst[1].n)
                gp.wait_ge(sks.h, sks.n)
                sCC.inc(gp.collective_compute(
                    "AllGather", ALU.bypass,
                    replica_groups=[[0, 1, 2, 3], [4, 5, 6, 7]],
                    ins=[k_loc], outs=[k_all]))
                sCC.inc(gp.collective_compute(
                    "AllGather", ALU.bypass,
                    replica_groups=[[0, 1, 2, 3], [4, 5, 6, 7]],
                    ins=[v_loc], outs=[v_all]))
        CC_DONE = 2
        if _DEBUG[0]:
            @block.sync
            def _(sy):
                sy.wait_ge(sks.h, sks.n)
                sy.wait_ge(svst[0].h, svst[0].n)
                sy.wait_ge(svst[1].h, svst[1].n)
                sdbg.inc(sy.dma_start(out=dbg_k, in_=k_loc), 16)
                sdbg.inc(sy.dma_start(out=dbg_v, in_=v_loc), 16)

        s_p2.close()

        s_p3 = ExitStack()
        Hs = s_p3.enter_context(nc.sbuf_tensor("Hs", [128, 64, 512], BF16))
        Hq = s_p3.enter_context(nc.sbuf_tensor("Hq", [128, 4, 512], F32))
        sqq = s_p3.enter_context(nc.sbuf_tensor("sqq", [128, 4, 512], F32))
        mx1 = s_p3.enter_context(nc.sbuf_tensor("mx1", [128, 512], F32))
        mx2 = s_p3.enter_context(nc.sbuf_tensor("mx2", [128, 512], F32))
        nn1 = s_p3.enter_context(nc.sbuf_tensor("nn1", [128, 512], F32))
        nn2 = s_p3.enter_context(nc.sbuf_tensor("nn2", [128, 512], F32))
        thr = s_p3.enter_context(nc.sbuf_tensor("thr", [128, 512], F32))
        msk = s_p3.enter_context(nc.sbuf_tensor("msk", [128, 512], F32))
        hG = [s_p3.enter_context(nc.sbuf_tensor(f"hG{i}", [128, 512], F32)) for i in range(2)]
        g_stage = [s_p3.enter_context(nc.sbuf_tensor(f"gstg{i}", [128, 512], BF16)) for i in range(2)]

        g_store_mark = [0] * 16
        down_end = [0]

        @block.vector
        def _(v):
            v.wait_ge(svst[0].h, svst[0].n)
            v.wait_ge(svst[1].h, svst[1].n)
            v.wait_ge(sks.h, sks.n)

        @block.scalar
        def _(s):
            s.wait_ge(svst[0].h, svst[0].n)
            s.wait_ge(svst[1].h, svst[1].n)
            s.wait_ge(sks.h, sks.n)

        for half in range(2):
            SE3 = sSE.n
            spars_mark = [0] * 16
            gu_g_done = [0] * 64
            gu_u_done = [0] * 64
            mul_mark = [0] * 64
            y_mark = [0] * 16

            for gchunk in range(16):
                for r in range(4):
                    n = gchunk * 4 + r
                    slot = (4 + n if n < 2 else (n - 2) % 4) if half == 0 else n % 4

                    @block.gpsimd
                    def _(gp, half=half, n=n):
                        def gu_load(un, sl):
                            cc = (un % 4) * 16 + un // 4
                            wait_free(gp, f"wslot{sl}")
                            sw[sl].inc(gp.dma_start(
                                out=wpool[:, sl, 0:2048].rearrange("p (k c) -> p k c", c=128),
                                in_=wg_blk[cc, :, :, :]), 16)
                            wld[("gu", half, un)] = sw[sl].inc(gp.dma_start(
                                out=wpool[:, sl, 2048:4096].rearrange("p (k c) -> p k c", c=128),
                                in_=wu_blk[cc, :, :, :]), 16)
                        if half == 0:
                            if n == 0:
                                for j in range(2, 5):
                                    gu_load(j, (j - 2) % 4)
                            if n + 3 < 64 and n + 3 >= 5:
                                gu_load(n + 3, (n + 3 - 2) % 4)
                        else:
                            if n == 0:
                                for j in range(3):
                                    gu_load(j, j)
                            if n + 3 < 64:
                                gu_load(n + 3, (n + 3) % 4)

                    if half == 0 and n == 1:
                        emit_collectives()

                    @block.tensor
                    def _(t, half=half, n=n, slot=slot):
                        gv = wpool[:, slot, 0:2048].rearrange("p (k c) -> p k c", c=128)
                        uv = wpool[:, slot, 2048:4096].rearrange("p (k c) -> p k c", c=128)
                        t.wait_ge(sw[slot].h, wld[("gu", half, n)])
                        wait_free(t, "pa0")
                        ins = None
                        for kc in range(NKC):
                            ins = t.matmul(pa[0][:, :], gv[:, kc, :],
                                           xnT[:, kc, half * 512:(half + 1) * 512],
                                           start=(kc == 0), stop=(kc == NKC - 1))
                        gu_g_done[n] = sPE.inc(ins)
                        wait_free(t, "pa1")
                        ins = None
                        for kc in range(NKC):
                            ins = t.matmul(pa[1][:, :], uv[:, kc, :],
                                           xnT[:, kc, half * 512:(half + 1) * 512],
                                           start=(kc == 0), stop=(kc == NKC - 1))
                        m = sPE.inc(ins)
                        gu_u_done[n] = m
                        set_free(f"wslot{slot}", sPE, m)
                    set_free("pa0", sSE, SE3 + n + 1)

                    @block.vector
                    def _(v, half=half, n=n, r=r, SE3=SE3):
                        v.wait_ge(sPE.h, gu_u_done[n])
                        v.wait_ge(sSE.h, SE3 + n + 1)
                        m = sVE.inc(v.tensor_tensor(Hq[:, r, :], pa[1][:, :], hG[n % 2][:, :], ALU.mult))
                        set_free("pa1", sVE, m)
                        mul_mark[n] = m

                @block.vector
                def _(v, half=half, gchunk=gchunk):
                    if half == 1 and gchunk == 0:
                        v.wait_ge(sPE.h, down_end[0])
                        if _DEBUG[0]:
                            wait_free(v, "hs_dump")
                    s4 = [sqq[:, r, :] for r in range(4)]
                    sVE.inc(v.tensor_tensor(sqq[:, :, :], Hq[:, :, :], Hq[:, :, :], ALU.mult))
                    v.wait_ge(sVE.h, sVE.n)
                    v.tensor_tensor(mx1[:, :], s4[0], s4[1], ALU.max)
                    v.tensor_tensor(nn1[:, :], s4[0], s4[1], ALU.min)
                    v.tensor_tensor(mx2[:, :], s4[2], s4[3], ALU.max)
                    sVE.inc(v.tensor_tensor(nn2[:, :], s4[2], s4[3], ALU.min))
                    v.wait_ge(sVE.h, sVE.n)
                    v.tensor_tensor(mx1[:, :], mx1[:, :], mx2[:, :], ALU.min)
                    sVE.inc(v.tensor_tensor(nn1[:, :], nn1[:, :], nn2[:, :], ALU.max))
                    v.wait_ge(sVE.h, sVE.n)
                    sVE.inc(v.tensor_tensor(thr[:, :], mx1[:, :], nn1[:, :], ALU.max))
                    v.wait_ge(sVE.h, sVE.n)
                    for r in range(4):
                        v.tensor_tensor(msk[:, :], sqq[:, r, :], thr[:, :], ALU.is_ge)
                        sVE.inc(v.tensor_tensor(Hs[:, r * 16 + gchunk, :], Hq[:, r, :],
                                                msk[:, :], ALU.mult))
                spars_mark[gchunk] = sVE.n
            HS_DONE = sVE.n
            if _DEBUG[0] and half == 0:
                @block.sync
                def _(sy):
                    sy.wait_ge(sVE.h, HS_DONE)
                    sdbg.inc(sy.dma_start(out=dbg_hs, in_=Hs[:, :, :]), 16)
                res["hs_dump"] = (sdbg, sdbg.n)

            for dchunk in range(16):
                dpar = dchunk % 2
                for h2 in range(2):
                    du = dchunk * 2 + h2
                    slot = du % 4

                    @block.gpsimd
                    def _(gp, half=half, du=du):
                        def d_load(un, sl):
                            wait_free(gp, f"wslot{sl}")
                            wld[("dn", half, un)] = sw[sl].inc(gp.dma_start(
                                out=wpool[:, sl, :].rearrange("p (k c) -> p k c", c=128),
                                in_=wd_blk[un // 2, :, (un % 2) * 32:(un % 2) * 32 + 32, :]), 16)
                        if du == 0:
                            for j in range(3):
                                d_load(j, j)
                        if du + 3 < 32:
                            d_load(du + 3, (du + 3) % 4)

                    if h2 == 1:
                        continue

                    @block.tensor
                    def _(t, half=half, dchunk=dchunk, du=du, dpar=dpar,
                          spars_mark=spars_mark):
                        slot0, slot1 = du % 4, (du + 1) % 4
                        dv0 = wpool[:, slot0, :].rearrange("p (k c) -> p k c", c=128)
                        dv1 = wpool[:, slot1, :].rearrange("p (k c) -> p k c", c=128)
                        t.wait_ge(sw[slot0].h, wld[("dn", half, du)])
                        t.wait_ge(sw[slot1].h, wld[("dn", half, du + 1)])
                        wait_free(t, f"pa{dpar}")
                        ins = None
                        first = True
                        for g in range(16):
                            if dchunk < 2:
                                t.wait_ge(sVE.h, spars_mark[g])
                            for r in range(4):
                                ic = r * 16 + g
                                ins = t.matmul(pa[dpar][:, :], (dv0 if ic < 32 else dv1)[:, ic % 32, :],
                                               Hs[:, ic, :], start=first, stop=(g == 15 and r == 3))
                                first = False
                        m = sPE.inc(ins)
                        set_free(f"wslot{du % 4}", sPE, m)
                        set_free(f"wslot{(du + 1) % 4}", sPE, m)
                y_mark[dchunk] = sPE.n
                down_end[0] = sPE.n
                set_free(f"pa{dpar}", sSE, SE3 + 64 + dchunk + 1)

                @block.sync
                def _(sy, half=half, dchunk=dchunk, dpar=dpar, SE3=SE3):
                    sy.wait_ge(sSE.h, SE3 + 64 + dchunk + 1)
                    m = sgst[dpar].inc(sy.dma_start(
                        out=g_sp[dchunk * 128:(dchunk + 1) * 128, half * 512:(half + 1) * 512],
                        in_=g_stage[dpar][:, :]), 16)
                    set_free(f"gstg{dpar}", sgst[dpar], m)
                    g_store_mark[dchunk] = (sgst[dpar], m)

            gst_free = [res.get("gstg0"), res.get("gstg1")]

            @block.scalar
            def _(s, half=half, SE3=SE3, gu_g_done=gu_g_done, mul_mark=mul_mark,
                  y_mark=y_mark, gst_free=gst_free):
                for n in range(64):
                    s.wait_ge(sPE.h, gu_g_done[n])
                    if n >= 2:
                        s.wait_ge(sVE.h, mul_mark[n - 2])
                    sSE.inc(s.activation(hG[n % 2][:, :], pa[0][:, :], AF.Silu))
                for dchunk in range(16):
                    s.wait_ge(sPE.h, y_mark[dchunk])
                    if dchunk >= 2:
                        ctr, val = gst_free[dchunk % 2]
                        s.wait_ge(ctr.h, val - 16 * ((16 + dchunk % 2 - dchunk) // 2))
                    sSE.inc(s.activation(g_stage[dchunk % 2][:, :], pa[dchunk % 2][:, :], AF.Silu))
                assert sSE.n == SE3 + 80, (sSE.n, SE3)
        s_p3.close()

        s_p4 = ExitStack()
        tmp4 = [s_p4.enter_context(nc.sbuf_tensor(f"tmp4_{i}", [128, 512], F32)) for i in range(2)]
        rp1_4 = s_p4.enter_context(nc.sbuf_tensor("rp1_4", [128, 512], F32))
        rp2_4 = s_p4.enter_context(nc.sbuf_tensor("rp2_4", [128, 512], F32))
        q_stage = [s_p4.enter_context(nc.sbuf_tensor(f"qstg{i}", [128, 512], BF16)) for i in range(2)]
        res.pop("tmp0", None)
        res.pop("tmp1", None)
        q_store_mark = [0] * NH

        @block.vector
        def _(v):
            v.wait_ge(sgst[0].h, sgst[0].n)
            v.wait_ge(sgst[1].h, sgst[1].n)

        @block.scalar
        def _(s):
            s.wait_ge(sgst[0].h, sgst[0].n)
            s.wait_ge(sgst[1].h, sgst[1].n)

        def p4_stage2(t, u):
            par = u["par"]
            t.wait_ge(sVE.h, u["tmp_done"])
            wait_free(t, f"pp{par}")
            u["pe2_done"] = sPE.inc(t.matmul(pp[par][:, :], rott_sb[:, :], tmp4[par][:, :],
                                             start=True, stop=True))

        def p4_post(u):
            par, tt2, qh = u["par"], u["tt2"], u["qh"]

            @block.vector
            def _(v):
                cs = cos_sb[:, tt2 * 512:(tt2 + 1) * 512]
                sn = sin_sb[:, tt2 * 512:(tt2 + 1) * 512]
                v.tensor_tensor(rp1_4[:, :], tmp4[par][:, :], cs, ALU.mult)
                v.wait_ge(sPE.h, u["pe2_done"])
                v.tensor_tensor(rp2_4[:, :], pp[par][:, :], sn, ALU.mult)
                wait_free(v, f"qstage{par}")
                m = sVE.inc(v.tensor_tensor(q_stage[par][:, :], rp1_4[:, :], rp2_4[:, :], ALU.add))
                set_free(f"tmp{par}", sVE, m)
                set_free(f"pp{par}", sVE, m)
            rope_done = sVE.n

            @block.sync
            def _(sy):
                sy.wait_ge(sVE.h, rope_done)
                m = sqst[par].inc(sy.dma_start(
                    out=q_sp[qh * 128:(qh + 1) * 128, tt2 * 512:(tt2 + 1) * 512],
                    in_=q_stage[par][:, :]), 16)
                set_free(f"qstage{par}", sqst[par], m)
                q_store_mark[qh] = (qh + 1) * 16

        pending4 = [None]
        for qh in range(NH):
            slot = qh % 4

            @block.gpsimd
            def _(gp, qh=qh):
                if qh == 0:
                    for j in range(3):
                        wload(gp, wi_blk, 16 + j, j, ("wq", j))
                if qh + 3 < NH:
                    wload(gp, wi_blk, 16 + qh + 3, (qh + 3) % 4, ("wq", qh + 3))

            for tt2 in range(2):
                n2 = qh * 2 + tt2
                par = n2 % 2
                u = dict(par=par, tt2=tt2, qh=qh)

                @block.tensor
                def _(t, qh=qh, tt2=tt2, par=par, slot=slot):
                    wait_free(t, f"pa{par}")
                    ins = matmul_accum(t, pa[par][:, :], slot,
                                       lambda kc: xnT[:, kc, tt2 * 512:(tt2 + 1) * 512],
                                       wld[("wq", qh)])
                    m = sPE.inc(ins)
                    if tt2 == 1:
                        set_free(f"wslot{slot}", sPE, m)
                    if pending4[0] is not None:
                        p4_stage2(t, pending4[0])
                u["acc_done"] = sPE.n

                @block.vector
                def _(v, par=par, u=u):
                    v.wait_ge(sPE.h, u["acc_done"])
                    wait_free(v, f"tmp{par}")
                    m = sVE.inc(v.tensor_copy(tmp4[par][:, :], pa[par][:, :]))
                    set_free(f"pa{par}", sVE, m)
                u["tmp_done"] = sVE.n

                if pending4[0] is not None:
                    p4_post(pending4[0])
                pending4[0] = u

        @block.tensor
        def _(t):
            p4_stage2(t, pending4[0])
        p4_post(pending4[0])
        pending4[0] = None

        s_p4.close()
        s_p14.close()
        s_xnT.close()

        s_ao = ExitStack()
        ao = s_ao.enter_context(nc.sbuf_tensor("ao", [128, NH, TOK], BF16, side="right"))
        s_p5 = ExitStack()
        kbuf = [s_p5.enter_context(nc.sbuf_tensor(f"kb{i}", [128, 4, 8, 128], BF16, side="right"))
                for i in range(2)]
        vbuf = [s_p5.enter_context(nc.sbuf_tensor(f"vb{i}", [128, 4, 8, 128], BF16, side="right"))
                for i in range(2)]
        q_buf = [s_p5.enter_context(nc.sbuf_tensor(f"qb{i}", [128, TOK], BF16)) for i in range(3)]
        ptil = [s_p5.enter_context(nc.sbuf_tensor(f"pt{i}", [128, TOK], BF16)) for i in range(14)]
        inv_f = s_p5.enter_context(nc.sbuf_tensor("inv_f", [1, TOK], F32))
        inv_bf = s_p5.enter_context(nc.sbuf_tensor("inv_bf", [1, TOK], BF16))
        bcast_sb = s_p5.enter_context(nc.sbuf_tensor("bcast_sb", [128, TOK], F32))
        pscore = [s_p5.enter_context(nc.psum_tensor(f"psc{i}", [128, TOK], F32)) for i in range(2)]
        pout = s_p5.enter_context(nc.psum_tensor("pout", [128, TOK], F32))
        prow = s_p5.enter_context(nc.psum_tensor("prow", [128, TOK], F32))

        P4_VE_END = sVE.n
        A0, B0, EX0, MK0, VE5 = sA.n, sB.n, sEX.n, sMK.n, sVE.n
        PEB0 = sPEb.n
        LAG = 12
        PT = 14

        def emit_bcast(t, j):
            t.wait_ge(sVE.h, VE5 + j * 4 + 2)
            t.matmul(prow[:, 0:512], ones1_sb[0:1, :], inv_bf[0:1, 0:512],
                     start=True, stop=True)
            sPEb.inc(t.matmul(prow[:, 512:TOK], ones1_sb[0:1, :], inv_bf[0:1, 512:TOK],
                              start=True, stop=True))
        qload = [0] * NH
        kvload = [0] * NKV

        for kvh in range(NKV):
            @block.sync
            def _(sy, kvh=kvh):
                if kvh < 2:
                    sy.wait_ge(sCC.h, CC_DONE)
                if kvh >= 2:
                    sy.wait_ge(sB.h, B0 + ((kvh - 2) * 4 + 4) * 32)
                m = 0
                for cp in range(4):
                    kb = kvh * HD
                    skv[kvh % 2].inc(sy.dma_start(
                        out=kbuf[kvh % 2][:, cp, :, :],
                        in_=k_all[cp * NKV * HD + kb: cp * NKV * HD + kb + 128, :]
                        .rearrange("d (m j) -> d m j", j=128)), 16)
                    vb = kvh * TOK
                    m = skv[kvh % 2].inc(sy.dma_start(
                        out=vbuf[kvh % 2][:, cp, :, :],
                        in_=v_all[cp * NKV * TOK + vb: cp * NKV * TOK + vb + TOK, :]
                        .rearrange("(m j) d -> j m d", j=128)), 16)
                kvload[kvh] = m

            for hh in range(4):
                idx = kvh * 4 + hh
                h = idx

                @block.sync
                def _(sy, idx=idx):
                    for ld in ([0, 1, 2] if idx == 0 else ([idx + 2] if idx + 2 < NH else [])):
                        if ld >= 3:
                            sy.wait_ge(sA.h, A0 + (ld - 2) * 32)
                        sy.wait_ge(sqst[0].h, sqst[0].n)
                        sy.wait_ge(sqst[1].h, sqst[1].n)
                        qload[ld] = sqb[ld % 3].inc(sy.dma_start(
                            out=q_buf[ld % 3][:, :], in_=q_sp[ld * 128:(ld + 1) * 128, :]), 16)

                @block.tensor
                def _(t, kvh=kvh, hh=hh, idx=idx):
                    kv = kbuf[kvh % 2]
                    vv = vbuf[kvh % 2]
                    qv = q_buf[idx % 3]
                    if idx == 0:
                        t.wait_ge(sVE.h, P4_VE_END)
                    t.wait_ge(skv[kvh % 2].h, kvload[kvh])
                    t.wait_ge(sqb[idx % 3].h, qload[idx])

                    def emit_s1(kt):
                        e = idx * 32 + kt
                        mq = kt // 4
                        c0 = mq * 128
                        if e >= 2:
                            t.wait_ge(sEX.h, EX0 + e - 1)
                        if mq < 4:
                            t.matmul(pscore[e % 2][:, c0:512], kv[:, kt % 4, mq, :],
                                     qv[:, c0:512], start=True, stop=True)
                            sA.inc(t.matmul(pscore[e % 2][:, 512:TOK], kv[:, kt % 4, mq, :],
                                            qv[:, 512:TOK], start=True, stop=True))
                        else:
                            sA.inc(t.matmul(pscore[e % 2][:, c0:TOK], kv[:, kt % 4, mq, :],
                                            qv[:, c0:TOK], start=True, stop=True))

                    def emit_s2(kt):
                        e = idx * 32 + kt
                        mq = kt // 4
                        c0 = mq * 128
                        t.wait_ge(sMK.h, MK0 + e + 1)
                        if kt == 0 and idx > 0:
                            t.wait_ge(sVE.h, VE5 + idx * 4)
                        st_, sp_ = (kt == 0), (kt == 31)
                        if mq < 4:
                            t.matmul(pout[:, c0:512], vv[:, kt % 4, mq, :],
                                     ptil[e % PT][:, c0:512], start=st_, stop=sp_)
                        t.matmul(pout[:, max(c0, 512):TOK], vv[:, kt % 4, mq, :],
                                 ptil[e % PT][:, max(c0, 512):TOK], start=st_, stop=sp_)
                        if mq < 4:
                            t.matmul(prow[0:1, c0:512], onesc_sb[:, 0:1],
                                     ptil[e % PT][:, c0:512], start=st_, stop=sp_)
                        sB.inc(t.matmul(prow[0:1, max(c0, 512):TOK], onesc_sb[:, 0:1],
                                        ptil[e % PT][:, max(c0, 512):TOK], start=st_, stop=sp_))

                    for kt in range(32):
                        emit_s1(kt)
                        if kt == 8 and idx > 0:
                            emit_bcast(t, idx - 1)
                        if kt >= LAG:
                            emit_s2(kt - LAG)
                    for kt in range(32 - LAG, 32):
                        emit_s2(kt)
                    if idx == NH - 1:
                        emit_bcast(t, idx)

                @block.vector
                def _(v, idx=idx, h=h):
                    for kt in range(32):
                        e = idx * 32 + kt
                        mq = kt // 4
                        v.wait_ge(sEX.h, EX0 + e + 1)
                        sMK.inc(v.tensor_tensor(ptil[e % PT][:, mq * 128:(mq + 1) * 128],
                                                ptil[e % PT][:, mq * 128:(mq + 1) * 128],
                                                hostm_sb[:, kt % 4, :], ALU.mult))
                    v.wait_ge(sB.h, B0 + idx * 32 + 32)
                    sVE.inc(v.reciprocal(inv_f[0:1, :], prow[0:1, :]))
                    v.wait_ge(sVE.h, sVE.n)
                    sVE.inc(v.tensor_copy(inv_bf[0:1, :], inv_f[0:1, :]))
                    v.wait_ge(sPEb.h, PEB0 + idx + 1)
                    sVE.inc(v.tensor_copy(bcast_sb[:, :], prow[:, :]))
                    v.wait_ge(sVE.h, sVE.n)
                    sVE.inc(v.tensor_tensor(ao[:, h, :], pout[:, :], bcast_sb[:, :], ALU.mult))
                assert sVE.n == VE5 + (idx + 1) * 4, (sVE.n, VE5, idx)
        @block.scalar
        def _(s):
            for idx2 in range(NH):
                for kt in range(32):
                    e = idx2 * 32 + kt
                    mq = kt // 4
                    s.wait_ge(sA.h, A0 + e + 1)
                    if e >= PT:
                        s.wait_ge(sB.h, B0 + e - PT + 1)
                    sEX.inc(s.activation(ptil[e % PT][:, mq * 128:TOK],
                                         pscore[e % 2][:, mq * 128:TOK], AF.Exp,
                                         bias=ebias_sb[:, 0:1], scale=SCALE))

        AO_DONE = sVE.n
        assert sPEb.n == PEB0 + NH
        if _DEBUG[0]:
            @block.sync
            def _(sy):
                sy.wait_ge(sVE.h, AO_DONE)
                sdbg.inc(sy.dma_start(out=dbg_ao, in_=ao[:, :, :]), 16)
        assert sEX.n == EX0 + NH * 32 and sA.n == A0 + NH * 32 and sMK.n == MK0 + NH * 32
        s_p5.close()

        s_p67 = ExitStack()
        sg = s_p67.enter_context(nc.sbuf_tensor("sg", [128, 16, TOK], F32))
        ub = [s_p67.enter_context(nc.sbuf_tensor(f"ub{i}", [128, 512], F32)) for i in range(2)]
        ostg = [s_p67.enter_context(nc.sbuf_tensor(f"ostg{i}", [128, 512], F32)) for i in range(2)]
        pp7 = [s_p67.enter_context(nc.psum_tensor(f"pp7{i}", [128, 512], F32)) for i in range(2)]
        sg_mark = [0] * 16
        uload7 = [0] * 32
        s_p6 = ExitStack()
        g_buf = [s_p6.enter_context(nc.sbuf_tensor(f"gb{i}", [128, TOK], BF16)) for i in range(2)]
        ps6 = [s_p6.enter_context(nc.psum_tensor(f"ps6{i}", [128, TOK], F32)) for i in range(2)]

        gload = [0] * 16
        for dchunk in range(16):
            slot = dchunk % 4
            par = dchunk % 2

            @block.gpsimd
            def _(gp, dchunk=dchunk):
                def wo_load(un, sl):
                    wait_free(gp, f"wslot{sl}")
                    wld[("wo", un)] = sw[sl].inc(gp.dma_start(
                        out=wpool[:, sl, :].rearrange("p (k c) -> p k c", c=128),
                        in_=wo_blk[un, :, :, :]), 16)
                if dchunk == 0:
                    for j in range(3):
                        wo_load(j, j)
                if dchunk + 3 < 16:
                    wo_load(dchunk + 3, (dchunk + 3) % 4)

            @block.sync
            def _(sy, dchunk=dchunk, par=par):
                if dchunk < 2:
                    sy.wait_ge(sVE.h, AO_DONE)
                wait_free(sy, f"gb{par}")
                gsem, gval = g_store_mark[dchunk]
                sy.wait_ge(gsem.h, gval)
                gload[dchunk] = sgb[par].inc(sy.dma_start(
                    out=g_buf[par][:, :], in_=g_sp[dchunk * 128:(dchunk + 1) * 128, :]), 16)

            @block.tensor
            def _(t, dchunk=dchunk, slot=slot, par=par):
                wv = wpool[:, slot, :].rearrange("p (k c) -> p k c", c=128)
                t.wait_ge(sw[slot].h, wld[("wo", dchunk)])
                if dchunk == 0:
                    t.wait_ge(sVE.h, AO_DONE)
                wait_free(t, f"ps6{par}")
                ins = None
                for hx in range(NH):
                    t.matmul(ps6[par][:, 0:512], wv[:, hx, :], ao[:, hx, 0:512],
                             start=(hx == 0), stop=(hx == NH - 1))
                    ins = t.matmul(ps6[par][:, 512:TOK], wv[:, hx, :], ao[:, hx, 512:TOK],
                                   start=(hx == 0), stop=(hx == NH - 1))
                m = sPE.inc(ins)
                set_free(f"wslot{slot}", sPE, m)
            s_done = sPE.n

            @block.vector
            def _(v, dchunk=dchunk, par=par, s_done=s_done):
                v.wait_ge(sPE.h, s_done)
                v.wait_ge(sgb[par].h, gload[dchunk])
                m = sVE.inc(v.tensor_tensor(sg[:, dchunk, :], ps6[par][:, :], g_buf[par][:, :],
                                            ALU.mult))
                set_free(f"ps6{par}", sVE, m)
                set_free(f"gb{par}", sVE, m)
                sg_mark[dchunk] = m

            if dchunk % 4 == 3:
                g4 = dchunk // 4
                for tt in range(8):
                    j = g4 * 8 + tt
                    jp = j % 2

                    @block.sync
                    def _(sy, g4=g4, tt=tt, j=j, jp=jp):
                        if j < 2:
                            sy.wait_ge(sVE.h, AO_DONE)
                        wait_free(sy, f"ub{jp}")
                        uload7[j] = su[jp].inc(sy.dma_start(
                            out=ub[jp][:, :],
                            in_=u_c[tt * 128:(tt + 1) * 128, g4 * 512:(g4 + 1) * 512]), 16)

                    @block.tensor
                    def _(t, g4=g4, tt=tt, jp=jp):
                        t.wait_ge(sVE.h, sg_mark[g4 * 4 + 3])
                        wait_free(t, f"pp7{jp}")
                        ins = None
                        for jj in range(4):
                            ins = t.transpose(pp7[jp][:, jj * 128:(jj + 1) * 128],
                                              sg[:, g4 * 4 + jj, tt * 128:(tt + 1) * 128],
                                              idf_sb[:, :])
                        sPE.inc(ins)
                    tr7 = sPE.n

                    @block.vector
                    def _(v, j=j, jp=jp, tr7=tr7):
                        v.wait_ge(sPE.h, tr7)
                        v.wait_ge(su[jp].h, uload7[j])
                        wait_free(v, f"ostg{jp}")
                        m7 = sVE.inc(v.tensor_tensor(ostg[jp][:, :], pp7[jp][:, :],
                                                     ub[jp][:, :], ALU.add))
                        set_free(f"pp7{jp}", sVE, m7)
                        set_free(f"ub{jp}", sVE, m7)
                    add7 = sVE.n

                    @block.sync
                    def _(sy, g4=g4, tt=tt, jp=jp, add7=add7):
                        sy.wait_ge(sVE.h, add7)
                        m7 = sost[jp].inc(sy.dma_start(
                            out=out_d[tt * 128:(tt + 1) * 128, g4 * 512:(g4 + 1) * 512],
                            in_=ostg[jp][:, :]), 16)
                        set_free(f"ostg{jp}", sost[jp], m7)
        SG_DONE = sVE.n
        if _DEBUG[0]:
            @block.sync
            def _(sy):
                sy.wait_ge(sVE.h, SG_DONE)
                sdbg.inc(sy.dma_start(out=dbg_sg, in_=sg[:, :, :]), 16)
        s_p6.close()
        s_ao.close()

        @block.sync
        def _(sy):
            for ctr in [sost[0], sost[1], sqst[0], sqst[1], sgst[0], sgst[1],
                        svst[0], svst[1], sks, sdbg]:
                if ctr.n:
                    sy.wait_ge(ctr.h, ctr.n)

        s_p67.close()

    return nc


_NC_CACHE = [None]


def _host_prep(u, W_in, W_out, w_gate, w_up, w_down):
    f32 = np.float32
    bf16 = ml_dtypes.bfloat16

    def blk(wT):
        K, C = wT.shape
        a = np.ascontiguousarray(wT).reshape(K // 128, 128, C // 128, 128)
        return np.ascontiguousarray(a.transpose(2, 1, 0, 3))

    ip = np.arange(IF)
    orig = (ip % 2048) * 4 + ip // 2048

    wi_blk = blk(np.ascontiguousarray(W_in[4096:10240].T))
    wo_blk = blk(np.ascontiguousarray(W_out.T))
    wg_blk = blk(np.ascontiguousarray(w_gate[:, orig]))
    wu_blk = blk(np.ascontiguousarray(w_up[:, orig]))
    wd_blk = blk(np.ascontiguousarray(w_down[orig, :]))

    R = np.zeros((128, 128), f32)
    for i in range(64):
        R[i, i + 64] = -1.0
        R[i + 64, i] = 1.0
    rott = np.ascontiguousarray(R.T)
    idf = np.eye(128, dtype=f32)
    onesc = np.ones((128, 1), bf16)
    ones1 = np.ones((1, 128), bf16)
    inv_freq = 1.0 / (10000.0 ** (np.arange(0, HD, 2, dtype=f32) / HD))

    per_core = []
    for c in range(8):
        bc, cz = c // 4, c % 4
        rows = np.concatenate([np.arange((4 * m + cz) * 128, (4 * m + cz) * 128 + 128)
                               for m in range(8)])
        u_cc = np.ascontiguousarray(u[bc][rows])
        pos = rows.astype(f32)
        freqs = pos[None, :] * inv_freq[:, None]
        cos_t = np.ascontiguousarray(np.concatenate([np.cos(freqs)] * 2, 0).astype(f32))
        sin_t = np.ascontiguousarray(np.concatenate([np.sin(freqs)] * 2, 0).astype(f32))
        tri = (np.arange(128)[:, None] <= np.arange(128)[None, :]).astype(f32)
        m4 = np.zeros((128, 4, 128), f32)
        for r in range(4):
            if r < cz:
                m4[:, r, :] = 1.0
            elif r == cz:
                m4[:, r, :] = tri
        per_core.append(dict(
            u_c=u_cc, wi_blk=wi_blk, wg_blk=wg_blk, wu_blk=wu_blk, wd_blk=wd_blk,
            wo_blk=wo_blk, cos_t=cos_t, sin_t=sin_t,
            hostm=np.ascontiguousarray(m4.astype(bf16)), rott=rott, idf=idf,
            onesc=onesc, ones1=ones1))
    return per_core


def kernel(L0_attn_u, L0_attn_W_in, L0_attn_W_out, L0_ffn_w_gate, L0_ffn_w_up, L0_ffn_w_down):
    in_maps = _host_prep(np.asarray(L0_attn_u, np.float32), np.asarray(L0_attn_W_in, np.float32),
                         np.asarray(L0_attn_W_out, np.float32), np.asarray(L0_ffn_w_gate, np.float32),
                         np.asarray(L0_ffn_w_up, np.float32), np.asarray(L0_ffn_w_down, np.float32))
    if _NC_CACHE[0] is None:
        _NC_CACHE[0] = build_program()
    res = run_bass_kernel_spmd(_NC_CACHE[0], in_maps, core_ids=list(range(8)))
    out = np.zeros((2, 4096, 2048), np.float32)
    for c in range(8):
        bc, cz = c // 4, c % 4
        rows = np.concatenate([np.arange((4 * m + cz) * 128, (4 * m + cz) * 128 + 128)
                               for m in range(8)])
        out[bc][rows] = np.asarray(res.results[c]["out"], np.float32)
    return out


if __name__ == "__main__":
    nc = build_program()
    try:
        print("instructions:", nc.m.n_instructions())
    except Exception:
        pass
    print("built ok")


# version 18
# speedup vs baseline: 1.0454x; 1.0454x over previous
import math
import numpy as np
import ml_dtypes
import concourse.bass as bass
import concourse.mybir as mybir
from concourse.bass_utils import run_bass_kernel_spmd
from contextlib import ExitStack

F32 = mybir.dt.float32
BF16 = mybir.dt.bfloat16
AF = mybir.ActivationFunctionType
ALU = mybir.AluOpType
AX = mybir.AxisListType

D = 2048
DQ = 4096
NH = 32
NKV = 8
HD = 128
IF = 8192
TOK = 1024
NKC = D // 128
EPS = 1e-5
EXP_BIAS = -12.0
SCALE = 1.0 / math.sqrt(HD)


class Ctr:
    def __init__(self, h):
        self.h = h
        self.n = 0

    def inc(self, ins, k=1):
        ins.then_inc(self.h, k)
        self.n += k
        return self.n


_DEBUG = [False]


def build_program():
    nc = bass.Bass("TRN2", target_bir_lowering=False)

    u_c = nc.dram_tensor("u_c", [TOK, D], F32, kind="ExternalInput").ap()
    wi_blk = nc.dram_tensor("wi_blk", [48, 128, NKC, 128], F32, kind="ExternalInput").ap()
    wg_blk = nc.dram_tensor("wg_blk", [64, 128, NKC, 128], F32, kind="ExternalInput").ap()
    wu_blk = nc.dram_tensor("wu_blk", [64, 128, NKC, 128], F32, kind="ExternalInput").ap()
    wd_blk = nc.dram_tensor("wd_blk", [16, 128, 64, 128], F32, kind="ExternalInput").ap()
    wo_blk = nc.dram_tensor("wo_blk", [16, 128, 32, 128], F32, kind="ExternalInput").ap()
    cos_t = nc.dram_tensor("cos_t", [128, TOK], F32, kind="ExternalInput").ap()
    sin_t = nc.dram_tensor("sin_t", [128, TOK], F32, kind="ExternalInput").ap()
    hostm = nc.dram_tensor("hostm", [128, 4, 128], BF16, kind="ExternalInput").ap()
    rott = nc.dram_tensor("rott", [128, 128], F32, kind="ExternalInput").ap()
    idf = nc.dram_tensor("idf", [128, 128], F32, kind="ExternalInput").ap()
    onesc = nc.dram_tensor("onesc", [128, 1], BF16, kind="ExternalInput").ap()
    ones1 = nc.dram_tensor("ones1", [1, 128], BF16, kind="ExternalInput").ap()
    out_d = nc.dram_tensor("out", [TOK, D], F32, kind="ExternalOutput").ap()

    k_loc = nc.dram_tensor("k_loc", [NKV * HD, TOK], BF16).ap()
    v_loc = nc.dram_tensor("v_loc", [NKV * TOK, HD], BF16).ap()
    k_all = nc.dram_tensor("k_all", [4 * NKV * HD, TOK], BF16).ap()
    v_all = nc.dram_tensor("v_all", [4 * NKV * TOK, HD], BF16).ap()
    q_sp = nc.dram_tensor("q_sp", [DQ, TOK], BF16).ap()
    g_sp = nc.dram_tensor("g_sp", [D, TOK], BF16).ap()
    hs_sp = nc.dram_tensor("hs_sp", [IF, 512], BF16).ap()
    if _DEBUG[0]:
        dbg_xnT = nc.dram_tensor("dbg_xnT", [128, NKC, TOK], BF16, kind="ExternalOutput").ap()
        dbg_k = nc.dram_tensor("dbg_k", [NKV * HD, TOK], BF16, kind="ExternalOutput").ap()
        dbg_v = nc.dram_tensor("dbg_v", [NKV * TOK, HD], BF16, kind="ExternalOutput").ap()
        dbg_q = nc.dram_tensor("dbg_q", [DQ, TOK], BF16, kind="ExternalOutput").ap()
        dbg_g = nc.dram_tensor("dbg_g", [D, TOK], BF16, kind="ExternalOutput").ap()
        dbg_ao = nc.dram_tensor("dbg_ao", [128, NH, TOK], BF16, kind="ExternalOutput").ap()
        dbg_hs = nc.dram_tensor("dbg_hs", [128, 64, 512], BF16, kind="ExternalOutput").ap()
        dbg_sg = nc.dram_tensor("dbg_sg", [128, 16, TOK], F32, kind="ExternalOutput").ap()

    top = ExitStack()
    with top:
        mksem = lambda name: Ctr(top.enter_context(nc.semaphore(name)))
        sCN = mksem("sCN")
        su = [mksem(f"su{i}") for i in range(2)]
        skv = [mksem(f"skv{i}") for i in range(2)]
        sqb = [mksem(f"sqb{i}") for i in range(3)]
        sgb = [mksem(f"sgb{i}") for i in range(2)]
        sw = [mksem(f"sw{i}") for i in range(6)]
        svst = [mksem(f"svst{i}") for i in range(2)]
        sks = mksem("sks")
        sgst = [mksem(f"sgst{i}") for i in range(2)]
        shs = [mksem(f"shs{i}") for i in range(2)]
        shr = mksem("shr")
        sqst = [mksem(f"sqst{i}") for i in range(2)]
        sost = [mksem(f"sost{i}") for i in range(2)]
        sdbg = mksem("sdbg")
        sPE = mksem("sPE")
        sPEb = mksem("sPEb")
        sVE = mksem("sVE")
        sSE = mksem("sSE")
        sCC = mksem("sCC")
        sA = mksem("sA")
        sB = mksem("sB")
        sEX = mksem("sEX")
        sMK = mksem("sMK")

        block = top.enter_context(nc.Block())

        res = {}

        def wait_free(eng, name):
            if name in res:
                ctr, val = res[name]
                eng.wait_ge(ctr.h, val)

        def set_free(name, ctr, val):
            res[name] = (ctr, val)

        cos_sb = top.enter_context(nc.sbuf_tensor("cos_sb", [128, TOK], F32))
        sin_sb = top.enter_context(nc.sbuf_tensor("sin_sb", [128, TOK], F32))
        hostm_sb = top.enter_context(nc.sbuf_tensor("hostm_sb", [128, 4, 128], BF16))
        rott_sb = top.enter_context(nc.sbuf_tensor("rott_sb", [128, 128], F32))
        idf_sb = top.enter_context(nc.sbuf_tensor("idf_sb", [128, 128], F32))
        onesc_sb = top.enter_context(nc.sbuf_tensor("onesc_sb", [128, 1], BF16))
        ones1_sb = top.enter_context(nc.sbuf_tensor("ones1_sb", [1, 128], BF16))
        ebias_sb = top.enter_context(nc.sbuf_tensor("ebias_sb", [128, 1], F32))
        eps_sb = top.enter_context(nc.sbuf_tensor("eps_sb", [128, 1], F32))
        wpool = top.enter_context(nc.sbuf_tensor("wpool", [128, 6, 4096], BF16))

        def wslot(i, nk):
            return wpool[:, i, :].rearrange("p (k c) -> p k c", c=128)[:, 0:nk, :]

        @block.sync
        def _(sy):
            for dst, src in [(cos_sb[:, :], cos_t), (sin_sb[:, :], sin_t),
                             (hostm_sb[:, :, :], hostm), (rott_sb[:, :], rott),
                             (idf_sb[:, :], idf), (onesc_sb[:, :], onesc),
                             (ones1_sb[:, :], ones1)]:
                sCN.inc(sy.dma_start(out=dst, in_=src), 16)
        CONST_LD = sCN.n

        @block.vector
        def _(v):
            v.memset(ebias_sb[:, :], EXP_BIAS)
            sVE.inc(v.memset(eps_sb[:, :], EPS))
        CONST_VE = sVE.n

        @block.tensor
        def _(t):
            t.wait_ge(sCN.h, CONST_LD)

        @block.vector
        def _(v):
            v.wait_ge(sCN.h, CONST_LD)

        @block.scalar
        def _(s):
            s.wait_ge(sCN.h, CONST_LD)

        s_xnT = ExitStack()
        xnT = s_xnT.enter_context(nc.sbuf_tensor("xnT", [128, NKC, TOK], BF16))
        s_p14 = ExitStack()
        pa = [s_p14.enter_context(nc.psum_tensor(f"pa{i}", [128, 512], F32)) for i in range(2)]
        pp = [s_p14.enter_context(nc.psum_tensor(f"pp{i}", [128, 512], F32)) for i in range(2)]

        s_p1 = ExitStack()
        u_buf = [s_p1.enter_context(nc.sbuf_tensor(f"u{i}", [128, D], F32)) for i in range(2)]
        sq_buf = [s_p1.enter_context(nc.sbuf_tensor("sq0", [128, D], F32))]
        xf_buf = [s_p1.enter_context(nc.sbuf_tensor(f"xf{i}", [128, D], F32)) for i in range(2)]
        st_buf = s_p1.enter_context(nc.sbuf_tensor("st", [128, 18], F32))

        uload = [0] * 8
        xnt_tt_mark = [0] * 8

        @block.sync
        def _(sy):
            for t in range(2):
                uload[t] = su[t].inc(sy.dma_start(out=u_buf[t][:, :], in_=u_c[t * 128:(t + 1) * 128, :]), 16)

        SE1 = sSE.n
        xf_free_mark = [0] * 8
        e2_mark = [0] * 8
        var_mark = [0] * 8
        bias_mark = [0] * 8
        mean_mark = [0] * 8

        for tt in range(8):
            par = tt % 2
            B = par * 9

            @block.vector
            def _(v, tt=tt, par=par, B=B):
                v.wait_ge(su[par].h, uload[tt])
                sVE.inc(v.reduce_sum(st_buf[:, B + 0:B + 1], u_buf[par][:, :], axis=AX.X))
                v.wait_ge(sVE.h, sVE.n)
                mean_mark[tt] = sVE.inc(v.tensor_scalar_mul(st_buf[:, B + 1:B + 2],
                                                            st_buf[:, B + 0:B + 1], 1.0 / D))
                v.wait_ge(sSE.h, SE1 + 3 * tt + 1)
                sVE.inc(v.tensor_scalar_mul(st_buf[:, B + 3:B + 4], st_buf[:, B + 2:B + 3], 1.0 / D))
                e2_mark[tt] = sVE.n
                sVE.inc(v.tensor_tensor(st_buf[:, B + 4:B + 5], st_buf[:, B + 1:B + 2],
                                        st_buf[:, B + 1:B + 2], ALU.mult))
                v.wait_ge(sVE.h, sVE.n)
                var_mark[tt] = sVE.inc(v.tensor_tensor(st_buf[:, B + 5:B + 6], st_buf[:, B + 3:B + 4],
                                                       st_buf[:, B + 4:B + 5], ALU.subtract))
                v.wait_ge(sSE.h, SE1 + 3 * tt + 2)
                sVE.inc(v.reciprocal(st_buf[:, B + 7:B + 8], st_buf[:, B + 6:B + 7]))
                v.wait_ge(sVE.h, sVE.n)
                bias_mark[tt] = sVE.inc(v.scalar_tensor_tensor(
                    st_buf[:, B + 8:B + 9], st_buf[:, B + 1:B + 2], -1.0,
                    st_buf[:, B + 7:B + 8], ALU.mult, ALU.mult))

            @block.sync
            def _(sy, tt=tt, par=par):
                if tt + 2 < 8:
                    sy.wait_ge(sVE.h, mean_mark[tt])
                    sy.wait_ge(sSE.h, SE1 + 3 * tt + 3)
                    uload[tt + 2] = su[par].inc(
                        sy.dma_start(out=u_buf[par][:, :], in_=u_c[(tt + 2) * 128:(tt + 3) * 128, :]), 16)

            for grp in range(4):
                gpar = grp % 2

                @block.tensor
                def _(t, tt=tt, grp=grp, gpar=gpar):
                    t.wait_ge(sSE.h, SE1 + 3 * tt + 3)
                    wait_free(t, f"pp{gpar}")
                    ins = None
                    for j in range(4):
                        kc = grp * 4 + j
                        ins = t.transpose(pp[gpar][:, j * 128:(j + 1) * 128],
                                          xf_buf[tt % 2][:, kc * 128:(kc + 1) * 128], idf_sb[:, :])
                    sPE.inc(ins)
                tr_done = sPE.n

                @block.vector
                def _(v, tt=tt, grp=grp, gpar=gpar, tr_done=tr_done):
                    v.wait_ge(sPE.h, tr_done)
                    m = sVE.inc(v.tensor_copy(
                        xnT[:, grp * 4:(grp + 1) * 4, tt * 128:(tt + 1) * 128],
                        pp[gpar][:, :].rearrange("p (j c) -> p j c", c=128)))
                    set_free(f"pp{gpar}", sVE, m)
                    if grp == 3:
                        xf_free_mark[tt] = m
            xnt_tt_mark[tt] = sVE.n

        @block.scalar
        def _(s):
            s.wait_ge(sVE.h, CONST_VE)
            for tt in range(8):
                par = tt % 2
                B = par * 9
                s.wait_ge(su[par].h, uload[tt])
                if tt >= 2:
                    s.wait_ge(sVE.h, e2_mark[tt - 2])
                sSE.inc(s.activation(sq_buf[0][:, :], u_buf[par][:, :], AF.Square,
                                     accum_out=st_buf[:, B + 2:B + 3]))
                s.wait_ge(sVE.h, var_mark[tt])
                sSE.inc(s.activation(st_buf[:, B + 6:B + 7], st_buf[:, B + 5:B + 6], AF.Sqrt,
                                     bias=eps_sb[:, 0:1], scale=1.0))
                s.wait_ge(sVE.h, bias_mark[tt])
                if tt >= 2:
                    s.wait_ge(sVE.h, xf_free_mark[tt - 2])
                sSE.inc(s.activation(xf_buf[par][:, :], u_buf[par][:, :], AF.Identity,
                                     bias=st_buf[:, B + 8:B + 9], scale=st_buf[:, B + 7:B + 8]))
            assert sSE.n == SE1 + 24

        XNT_DONE = ("VE", sVE.n)
        if _DEBUG[0]:
            @block.sync
            def _(sy):
                sy.wait_ge(sVE.h, XNT_DONE[1])
                sdbg.inc(sy.dma_start(out=dbg_xnT, in_=xnT[:, :, :]), 16)
        s_p1.close()

        s_p2 = ExitStack()
        k_stage = s_p2.enter_context(nc.sbuf_tensor("k_stage", [128, NKV, TOK], BF16))
        tmp2 = [s_p2.enter_context(nc.sbuf_tensor(f"tmp2_{i}", [128, 512], F32)) for i in range(2)]
        rp1_2 = s_p2.enter_context(nc.sbuf_tensor("rp1_2", [128, 512], F32))
        rp2_2 = s_p2.enter_context(nc.sbuf_tensor("rp2_2", [128, 512], F32))
        v_stage = [s_p2.enter_context(nc.sbuf_tensor(f"vst{i}", [128, 512], BF16)) for i in range(2)]

        wld = {}
        hs1_mark = [0] * 64

        def wload(gp, blk, bidx, slot, key):
            wait_free(gp, f"wslot{slot}")
            wld[key] = sw[slot].inc(gp.dma_start(out=wslot(slot, NKC), in_=blk[bidx, :, :, :]), 16)

        def matmul_accum(t, psum_ap, slot, rhs_fn, wmark):
            t.wait_ge(sw[slot].h, wmark)
            ins = None
            for kc in range(NKC):
                ins = t.matmul(psum_ap, wslot(slot, NKC)[:, kc, :], rhs_fn(kc),
                               start=(kc == 0), stop=(kc == NKC - 1))
            return ins

        kst_last = [0]

        def p2_stage2(t, u):
            par = u["par"]
            t.wait_ge(sVE.h, u["tmp_done"])
            wait_free(t, f"pp{par}")
            if u["is_v"]:
                ins = None
                for j in range(4):
                    ins = t.transpose(pp[par][:, j * 128:(j + 1) * 128],
                                      tmp2[par][:, j * 128:(j + 1) * 128], idf_sb[:, :])
            else:
                ins = t.matmul(pp[par][:, :], rott_sb[:, :], tmp2[par][:, :],
                               start=True, stop=True)
            u["pe2_done"] = sPE.inc(ins)

        def p2_post(u):
            par, tt2, kvh = u["par"], u["tt2"], u["kvh"]
            if u["is_v"]:
                @block.vector
                def _(v):
                    v.wait_ge(sPE.h, u["pe2_done"])
                    wait_free(v, f"vst{par}")
                    m = sVE.inc(v.tensor_copy(v_stage[par][:, :], pp[par][:, :]))
                    set_free(f"pp{par}", sVE, m)
                    set_free(f"tmp{par}", sVE, m)
                vev_done = sVE.n

                @block.sync
                def _(sy):
                    sy.wait_ge(sVE.h, vev_done)
                    base = kvh * TOK + tt2 * 512
                    dst = v_loc[base:base + 512, :].rearrange("(jt j) d -> j jt d", j=128)
                    m = svst[par].inc(sy.dma_start(
                        out=dst, in_=v_stage[par][:, :].rearrange("j (jt d) -> j jt d", d=128)), 16)
                    set_free(f"vst{par}", svst[par], m)
            else:
                @block.vector
                def _(v):
                    cs = cos_sb[:, tt2 * 512:(tt2 + 1) * 512]
                    sn = sin_sb[:, tt2 * 512:(tt2 + 1) * 512]
                    v.tensor_tensor(rp1_2[:, :], tmp2[par][:, :], cs, ALU.mult)
                    v.wait_ge(sPE.h, u["pe2_done"])
                    v.tensor_tensor(rp2_2[:, :], pp[par][:, :], sn, ALU.mult)
                    m = sVE.inc(v.tensor_tensor(k_stage[:, kvh, tt2 * 512:(tt2 + 1) * 512],
                                                rp1_2[:, :], rp2_2[:, :], ALU.add))
                    set_free(f"tmp{par}", sVE, m)
                    set_free(f"pp{par}", sVE, m)
                    kst_last[0] = m

        pending = [None]
        for oc in range(16):
            slot = oc % 4

            @block.gpsimd
            def _(gp, oc=oc):
                if oc == 0:
                    for j in range(3):
                        wload(gp, wi_blk, j, j, ("wi", j))
                if oc == 1:
                    for un, sl in [(0, 4), (1, 5)]:
                        cc = (un % 4) * 16 + un // 4
                        sw[sl].inc(gp.dma_start(
                            out=wpool[:, sl, 0:2048].rearrange("p (k c) -> p k c", c=128),
                            in_=wg_blk[cc, :, :, :]), 16)
                        wld[("gu", 0, un)] = sw[sl].inc(gp.dma_start(
                            out=wpool[:, sl, 2048:4096].rearrange("p (k c) -> p k c", c=128),
                            in_=wu_blk[cc, :, :, :]), 16)
                if oc + 3 < 16:
                    wload(gp, wi_blk, oc + 3, (oc + 3) % 4, ("wi", oc + 3))

            for tt2 in range(2):
                n2 = oc * 2 + tt2
                par = n2 % 2
                u = dict(par=par, tt2=tt2, kvh=oc if oc < 8 else oc - 8, is_v=(oc < 8))

                @block.tensor
                def _(t, oc=oc, tt2=tt2, par=par, slot=slot):
                    if oc == 0 and tt2 == 0:
                        t.wait_ge(sVE.h, XNT_DONE[1])
                    wait_free(t, f"pa{par}")
                    ins = matmul_accum(t, pa[par][:, :], slot,
                                       lambda kc: xnT[:, kc, tt2 * 512:(tt2 + 1) * 512],
                                       wld[("wi", oc)])
                    m = sPE.inc(ins)
                    if tt2 == 1:
                        set_free(f"wslot{slot}", sPE, m)
                    if pending[0] is not None:
                        p2_stage2(t, pending[0])
                u["acc_done"] = sPE.n

                @block.vector
                def _(v, par=par, u=u):
                    v.wait_ge(sPE.h, u["acc_done"])
                    wait_free(v, f"tmp{par}")
                    m = sVE.inc(v.tensor_copy(tmp2[par][:, :], pa[par][:, :]))
                    set_free(f"pa{par}", sVE, m)
                u["tmp_done"] = sVE.n

                if pending[0] is not None:
                    p2_post(pending[0])
                pending[0] = u

        @block.tensor
        def _(t):
            p2_stage2(t, pending[0])
        p2_post(pending[0])
        pending[0] = None

        @block.sync
        def _(sy):
            sy.wait_ge(sVE.h, kst_last[0])
            sks.inc(sy.dma_start(out=k_loc.rearrange("(h d) t -> d h t", d=128),
                                 in_=k_stage[:, :, :]), 16)

        def emit_collectives():
            @block.gpsimd
            def _(gp):
                gp.wait_ge(svst[0].h, svst[0].n)
                gp.wait_ge(svst[1].h, svst[1].n)
                gp.wait_ge(sks.h, sks.n)
                sCC.inc(gp.collective_compute(
                    "AllGather", ALU.bypass,
                    replica_groups=[[0, 1, 2, 3], [4, 5, 6, 7]],
                    ins=[k_loc], outs=[k_all]))
                sCC.inc(gp.collective_compute(
                    "AllGather", ALU.bypass,
                    replica_groups=[[0, 1, 2, 3], [4, 5, 6, 7]],
                    ins=[v_loc], outs=[v_all]))
        CC_DONE = 2
        if _DEBUG[0]:
            @block.sync
            def _(sy):
                sy.wait_ge(sks.h, sks.n)
                sy.wait_ge(svst[0].h, svst[0].n)
                sy.wait_ge(svst[1].h, svst[1].n)
                sdbg.inc(sy.dma_start(out=dbg_k, in_=k_loc), 16)
                sdbg.inc(sy.dma_start(out=dbg_v, in_=v_loc), 16)

        s_p2.close()

        s_p3 = ExitStack()
        Hs = s_p3.enter_context(nc.sbuf_tensor("Hs", [128, 64, 512], BF16))
        Hq = s_p3.enter_context(nc.sbuf_tensor("Hq", [128, 4, TOK], F32))
        sqq = s_p3.enter_context(nc.sbuf_tensor("sqq", [128, 4, 512], F32))
        mx1 = s_p3.enter_context(nc.sbuf_tensor("mx1", [128, 512], F32))
        mx2 = s_p3.enter_context(nc.sbuf_tensor("mx2", [128, 512], F32))
        nn1 = s_p3.enter_context(nc.sbuf_tensor("nn1", [128, 512], F32))
        nn2 = s_p3.enter_context(nc.sbuf_tensor("nn2", [128, 512], F32))
        thr = s_p3.enter_context(nc.sbuf_tensor("thr", [128, 512], F32))
        msk = s_p3.enter_context(nc.sbuf_tensor("msk", [128, 512], F32))
        hG = [s_p3.enter_context(nc.sbuf_tensor(f"hG{i}", [128, TOK], F32)) for i in range(2)]
        hs1st = [s_p3.enter_context(nc.sbuf_tensor(f"hs1st{i}", [128, 512], BF16)) for i in range(2)]
        g_stage = [s_p3.enter_context(nc.sbuf_tensor(f"gstg{i}", [128, 512], BF16)) for i in range(2)]
        paw = [s_p3.enter_context(nc.psum_tensor(f"paw{i}", [128, TOK], F32)) for i in range(2)]

        g_store_mark = [0] * 16

        @block.vector
        def _(v):
            v.wait_ge(svst[0].h, svst[0].n)
            v.wait_ge(svst[1].h, svst[1].n)
            v.wait_ge(sks.h, sks.n)

        @block.scalar
        def _(s):
            s.wait_ge(svst[0].h, svst[0].n)
            s.wait_ge(svst[1].h, svst[1].n)
            s.wait_ge(sks.h, sks.n)

        SE3 = sSE.n
        spars_mark = [0] * 16
        gu_g_done = [0] * 64
        gu_u_done = [0] * 64
        mul_mark = [0] * 64
        y_mark = [[0] * 16, [0] * 16]

        for gchunk in range(16):
            for r in range(4):
                n = gchunk * 4 + r
                slot = 4 + n if n < 2 else (n - 2) % 4

                @block.gpsimd
                def _(gp, n=n):
                    def gu_load(un, sl):
                        cc = (un % 4) * 16 + un // 4
                        wait_free(gp, f"wslot{sl}")
                        sw[sl].inc(gp.dma_start(
                            out=wpool[:, sl, 0:2048].rearrange("p (k c) -> p k c", c=128),
                            in_=wg_blk[cc, :, :, :]), 16)
                        wld[("gu", 0, un)] = sw[sl].inc(gp.dma_start(
                            out=wpool[:, sl, 2048:4096].rearrange("p (k c) -> p k c", c=128),
                            in_=wu_blk[cc, :, :, :]), 16)
                    if n == 0:
                        for j in range(2, 5):
                            gu_load(j, (j - 2) % 4)
                    if n + 3 < 64 and n + 3 >= 5:
                        gu_load(n + 3, (n + 3 - 2) % 4)

                if n == 1:
                    emit_collectives()

                @block.tensor
                def _(t, n=n, slot=slot):
                    gv = wpool[:, slot, 0:2048].rearrange("p (k c) -> p k c", c=128)
                    uv = wpool[:, slot, 2048:4096].rearrange("p (k c) -> p k c", c=128)
                    t.wait_ge(sw[slot].h, wld[("gu", 0, n)])
                    wait_free(t, "paw0")
                    ins = None
                    for kc in range(NKC):
                        t.matmul(paw[0][:, 0:512], gv[:, kc, :], xnT[:, kc, 0:512],
                                 start=(kc == 0), stop=(kc == NKC - 1))
                        ins = t.matmul(paw[0][:, 512:TOK], gv[:, kc, :], xnT[:, kc, 512:TOK],
                                       start=(kc == 0), stop=(kc == NKC - 1))
                    gu_g_done[n] = sPE.inc(ins)
                    wait_free(t, "paw1")
                    ins = None
                    for kc in range(NKC):
                        t.matmul(paw[1][:, 0:512], uv[:, kc, :], xnT[:, kc, 0:512],
                                 start=(kc == 0), stop=(kc == NKC - 1))
                        ins = t.matmul(paw[1][:, 512:TOK], uv[:, kc, :], xnT[:, kc, 512:TOK],
                                       start=(kc == 0), stop=(kc == NKC - 1))
                    m = sPE.inc(ins)
                    gu_u_done[n] = m
                    set_free(f"wslot{slot}", sPE, m)
                set_free("paw0", sSE, SE3 + n + 1)

                @block.vector
                def _(v, n=n, r=r, SE3=SE3):
                    v.wait_ge(sPE.h, gu_u_done[n])
                    v.wait_ge(sSE.h, SE3 + n + 1)
                    m = sVE.inc(v.tensor_tensor(Hq[:, r, :], paw[1][:, :], hG[n % 2][:, :], ALU.mult))
                    set_free("paw1", sVE, m)
                    mul_mark[n] = m

            @block.vector
            def _(v, gchunk=gchunk):
                for hf in range(2):
                    o = hf * 512
                    s4 = [sqq[:, r, :] for r in range(4)]
                    sVE.inc(v.tensor_tensor(sqq[:, :, :], Hq[:, :, o:o + 512],
                                            Hq[:, :, o:o + 512], ALU.mult))
                    v.wait_ge(sVE.h, sVE.n)
                    v.tensor_tensor(mx1[:, :], s4[0], s4[1], ALU.max)
                    v.tensor_tensor(nn1[:, :], s4[0], s4[1], ALU.min)
                    v.tensor_tensor(mx2[:, :], s4[2], s4[3], ALU.max)
                    sVE.inc(v.tensor_tensor(nn2[:, :], s4[2], s4[3], ALU.min))
                    v.wait_ge(sVE.h, sVE.n)
                    v.tensor_tensor(mx1[:, :], mx1[:, :], mx2[:, :], ALU.min)
                    sVE.inc(v.tensor_tensor(nn1[:, :], nn1[:, :], nn2[:, :], ALU.max))
                    v.wait_ge(sVE.h, sVE.n)
                    sVE.inc(v.tensor_tensor(thr[:, :], mx1[:, :], nn1[:, :], ALU.max))
                    v.wait_ge(sVE.h, sVE.n)
                    for r in range(4):
                        v.tensor_tensor(msk[:, :], sqq[:, r, :], thr[:, :], ALU.is_ge)
                        if hf == 0:
                            sVE.inc(v.tensor_tensor(Hs[:, r * 16 + gchunk, :],
                                                    Hq[:, r, 0:512], msk[:, :], ALU.mult))
                        else:
                            wait_free(v, f"hs1st{r % 2}")
                            hs1_mark[gchunk * 4 + r] = sVE.inc(v.tensor_tensor(
                                hs1st[r % 2][:, :], Hq[:, r, 512:TOK], msk[:, :], ALU.mult))
            spars_mark[gchunk] = sVE.n

            for r in range(4):
                @block.sync
                def _(sy, gchunk=gchunk, r=r):
                    sy.wait_ge(sVE.h, hs1_mark[gchunk * 4 + r])
                    ic = r * 16 + gchunk
                    m = shs[r % 2].inc(sy.dma_start(
                        out=hs_sp[ic * 128:(ic + 1) * 128, :], in_=hs1st[r % 2][:, :]), 16)
                    set_free(f"hs1st{r % 2}", shs[r % 2], m)

        @block.scalar
        def _(s):
            for n in range(64):
                s.wait_ge(sPE.h, gu_g_done[n])
                if n >= 2:
                    s.wait_ge(sVE.h, mul_mark[n - 2])
                sSE.inc(s.activation(hG[n % 2][:, :], paw[0][:, :], AF.Silu))
            assert sSE.n == SE3 + 64

        for half in range(2):
            if half == 1:
                @block.sync
                def _(sy):
                    sy.wait_ge(sPE.h, y_mark[0][15])
                    for i in range(2):
                        sy.wait_ge(shs[i].h, shs[i].n)
                    for ic0 in range(0, 64, 8):
                        shr.inc(sy.dma_start(
                            out=Hs[:, ic0:ic0 + 8, :],
                            in_=hs_sp[ic0 * 128:(ic0 + 8) * 128, :]
                            .rearrange("(ic p) c -> p ic c", p=128)), 16)

            for dchunk in range(16):
                dpar = dchunk % 2
                for h2 in range(2):
                    du = dchunk * 2 + h2

                    @block.gpsimd
                    def _(gp, half=half, du=du):
                        def d_load(un, sl):
                            wait_free(gp, f"wslot{sl}")
                            wld[("dn", half, un)] = sw[sl].inc(gp.dma_start(
                                out=wpool[:, sl, :].rearrange("p (k c) -> p k c", c=128),
                                in_=wd_blk[un // 2, :, (un % 2) * 32:(un % 2) * 32 + 32, :]), 16)
                        if du == 0:
                            for j in range(3):
                                d_load(j, j)
                        if du + 3 < 32:
                            d_load(du + 3, (du + 3) % 4)

                    if h2 == 1:
                        continue

                    @block.tensor
                    def _(t, half=half, dchunk=dchunk, du=du, dpar=dpar):
                        slot0, slot1 = du % 4, (du + 1) % 4
                        dv0 = wpool[:, slot0, :].rearrange("p (k c) -> p k c", c=128)
                        dv1 = wpool[:, slot1, :].rearrange("p (k c) -> p k c", c=128)
                        t.wait_ge(sw[slot0].h, wld[("dn", half, du)])
                        t.wait_ge(sw[slot1].h, wld[("dn", half, du + 1)])
                        if half == 1 and dchunk == 0:
                            t.wait_ge(shr.h, shr.n)
                        wait_free(t, f"pa{dpar}")
                        ins = None
                        first = True
                        for g in range(16):
                            if half == 0 and dchunk < 2:
                                t.wait_ge(sVE.h, spars_mark[g])
                            for r in range(4):
                                ic = r * 16 + g
                                ins = t.matmul(pa[dpar][:, :],
                                               (dv0 if ic < 32 else dv1)[:, ic % 32, :],
                                               Hs[:, ic, :], start=first,
                                               stop=(g == 15 and r == 3))
                                first = False
                        m = sPE.inc(ins)
                        set_free(f"wslot{du % 4}", sPE, m)
                        set_free(f"wslot{(du + 1) % 4}", sPE, m)
                y_mark[half][dchunk] = sPE.n
                set_free(f"pa{dpar}", sSE, SE3 + 64 + half * 16 + dchunk + 1)

                @block.sync
                def _(sy, half=half, dchunk=dchunk, dpar=dpar, SE3=SE3):
                    sy.wait_ge(sSE.h, SE3 + 64 + half * 16 + dchunk + 1)
                    m = sgst[dpar].inc(sy.dma_start(
                        out=g_sp[dchunk * 128:(dchunk + 1) * 128, half * 512:(half + 1) * 512],
                        in_=g_stage[dpar][:, :]), 16)
                    set_free(f"gstg{dpar}", sgst[dpar], m)
                    g_store_mark[dchunk] = (sgst[dpar], m)

            gst_free = [res.get("gstg0"), res.get("gstg1")]

            @block.scalar
            def _(s, half=half, SE3=SE3, gst_free=gst_free):
                for dchunk in range(16):
                    s.wait_ge(sPE.h, y_mark[half][dchunk])
                    if half > 0 or dchunk >= 2:
                        ctr, val = gst_free[dchunk % 2]
                        back = ((14 + dchunk % 2) - (dchunk - 2)) // 2 if dchunk >= 2 else 8
                        s.wait_ge(ctr.h, val - 16 * back)
                    sSE.inc(s.activation(g_stage[dchunk % 2][:, :], pa[dchunk % 2][:, :], AF.Silu))
                assert sSE.n == SE3 + 64 + (half + 1) * 16
        s_p3.close()

        s_p4 = ExitStack()
        tmp4 = [s_p4.enter_context(nc.sbuf_tensor(f"tmp4_{i}", [128, 512], F32)) for i in range(2)]
        rp1_4 = s_p4.enter_context(nc.sbuf_tensor("rp1_4", [128, 512], F32))
        rp2_4 = s_p4.enter_context(nc.sbuf_tensor("rp2_4", [128, 512], F32))
        q_stage = [s_p4.enter_context(nc.sbuf_tensor(f"qstg{i}", [128, 512], BF16)) for i in range(2)]
        res.pop("tmp0", None)
        res.pop("tmp1", None)
        q_store_mark = [0] * NH

        @block.vector
        def _(v):
            v.wait_ge(sgst[0].h, sgst[0].n)
            v.wait_ge(sgst[1].h, sgst[1].n)

        @block.scalar
        def _(s):
            s.wait_ge(sgst[0].h, sgst[0].n)
            s.wait_ge(sgst[1].h, sgst[1].n)

        def p4_stage2(t, u):
            par = u["par"]
            t.wait_ge(sVE.h, u["tmp_done"])
            wait_free(t, f"pp{par}")
            u["pe2_done"] = sPE.inc(t.matmul(pp[par][:, :], rott_sb[:, :], tmp4[par][:, :],
                                             start=True, stop=True))

        def p4_post(u):
            par, tt2, qh = u["par"], u["tt2"], u["qh"]

            @block.vector
            def _(v):
                cs = cos_sb[:, tt2 * 512:(tt2 + 1) * 512]
                sn = sin_sb[:, tt2 * 512:(tt2 + 1) * 512]
                v.tensor_tensor(rp1_4[:, :], tmp4[par][:, :], cs, ALU.mult)
                v.wait_ge(sPE.h, u["pe2_done"])
                v.tensor_tensor(rp2_4[:, :], pp[par][:, :], sn, ALU.mult)
                wait_free(v, f"qstage{par}")
                m = sVE.inc(v.tensor_tensor(q_stage[par][:, :], rp1_4[:, :], rp2_4[:, :], ALU.add))
                set_free(f"tmp{par}", sVE, m)
                set_free(f"pp{par}", sVE, m)
            rope_done = sVE.n

            @block.sync
            def _(sy):
                sy.wait_ge(sVE.h, rope_done)
                m = sqst[par].inc(sy.dma_start(
                    out=q_sp[qh * 128:(qh + 1) * 128, tt2 * 512:(tt2 + 1) * 512],
                    in_=q_stage[par][:, :]), 16)
                set_free(f"qstage{par}", sqst[par], m)
                q_store_mark[qh] = (qh + 1) * 16

        pending4 = [None]
        for qh in range(NH):
            slot = qh % 4

            @block.gpsimd
            def _(gp, qh=qh):
                if qh == 0:
                    for j in range(3):
                        wload(gp, wi_blk, 16 + j, j, ("wq", j))
                if qh + 3 < NH:
                    wload(gp, wi_blk, 16 + qh + 3, (qh + 3) % 4, ("wq", qh + 3))

            for tt2 in range(2):
                n2 = qh * 2 + tt2
                par = n2 % 2
                u = dict(par=par, tt2=tt2, qh=qh)

                @block.tensor
                def _(t, qh=qh, tt2=tt2, par=par, slot=slot):
                    wait_free(t, f"pa{par}")
                    ins = matmul_accum(t, pa[par][:, :], slot,
                                       lambda kc: xnT[:, kc, tt2 * 512:(tt2 + 1) * 512],
                                       wld[("wq", qh)])
                    m = sPE.inc(ins)
                    if tt2 == 1:
                        set_free(f"wslot{slot}", sPE, m)
                    if pending4[0] is not None:
                        p4_stage2(t, pending4[0])
                u["acc_done"] = sPE.n

                @block.vector
                def _(v, par=par, u=u):
                    v.wait_ge(sPE.h, u["acc_done"])
                    wait_free(v, f"tmp{par}")
                    m = sVE.inc(v.tensor_copy(tmp4[par][:, :], pa[par][:, :]))
                    set_free(f"pa{par}", sVE, m)
                u["tmp_done"] = sVE.n

                if pending4[0] is not None:
                    p4_post(pending4[0])
                pending4[0] = u

        @block.tensor
        def _(t):
            p4_stage2(t, pending4[0])
        p4_post(pending4[0])
        pending4[0] = None

        s_p4.close()
        s_p14.close()
        s_xnT.close()

        s_ao = ExitStack()
        ao = s_ao.enter_context(nc.sbuf_tensor("ao", [128, NH, TOK], BF16, side="right"))
        s_p5 = ExitStack()
        kbuf = [s_p5.enter_context(nc.sbuf_tensor(f"kb{i}", [128, 4, 8, 128], BF16, side="right"))
                for i in range(2)]
        vbuf = [s_p5.enter_context(nc.sbuf_tensor(f"vb{i}", [128, 4, 8, 128], BF16, side="right"))
                for i in range(2)]
        q_buf = [s_p5.enter_context(nc.sbuf_tensor(f"qb{i}", [128, TOK], BF16)) for i in range(3)]
        ptil = [s_p5.enter_context(nc.sbuf_tensor(f"pt{i}", [128, TOK], BF16)) for i in range(14)]
        inv_f = s_p5.enter_context(nc.sbuf_tensor("inv_f", [1, TOK], F32))
        inv_bf = s_p5.enter_context(nc.sbuf_tensor("inv_bf", [1, TOK], BF16))
        bcast_sb = s_p5.enter_context(nc.sbuf_tensor("bcast_sb", [128, TOK], F32))
        pscore = [s_p5.enter_context(nc.psum_tensor(f"psc{i}", [128, TOK], F32)) for i in range(2)]
        pout = s_p5.enter_context(nc.psum_tensor("pout", [128, TOK], F32))
        prow = s_p5.enter_context(nc.psum_tensor("prow", [128, TOK], F32))

        P4_VE_END = sVE.n
        A0, B0, EX0, MK0, VE5 = sA.n, sB.n, sEX.n, sMK.n, sVE.n
        PEB0 = sPEb.n
        LAG = 12
        PT = 14

        def emit_bcast(t, j):
            t.wait_ge(sVE.h, VE5 + j * 4 + 2)
            t.matmul(prow[:, 0:512], ones1_sb[0:1, :], inv_bf[0:1, 0:512],
                     start=True, stop=True)
            sPEb.inc(t.matmul(prow[:, 512:TOK], ones1_sb[0:1, :], inv_bf[0:1, 512:TOK],
                              start=True, stop=True))
        qload = [0] * NH
        kvload = [0] * NKV

        for kvh in range(NKV):
            @block.sync
            def _(sy, kvh=kvh):
                if kvh < 2:
                    sy.wait_ge(sCC.h, CC_DONE)
                if kvh >= 2:
                    sy.wait_ge(sB.h, B0 + ((kvh - 2) * 4 + 4) * 32)
                m = 0
                for cp in range(4):
                    kb = kvh * HD
                    skv[kvh % 2].inc(sy.dma_start(
                        out=kbuf[kvh % 2][:, cp, :, :],
                        in_=k_all[cp * NKV * HD + kb: cp * NKV * HD + kb + 128, :]
                        .rearrange("d (m j) -> d m j", j=128)), 16)
                    vb = kvh * TOK
                    m = skv[kvh % 2].inc(sy.dma_start(
                        out=vbuf[kvh % 2][:, cp, :, :],
                        in_=v_all[cp * NKV * TOK + vb: cp * NKV * TOK + vb + TOK, :]
                        .rearrange("(m j) d -> j m d", j=128)), 16)
                kvload[kvh] = m

            for hh in range(4):
                idx = kvh * 4 + hh
                h = idx

                @block.sync
                def _(sy, idx=idx):
                    for ld in ([0, 1, 2] if idx == 0 else ([idx + 2] if idx + 2 < NH else [])):
                        if ld >= 3:
                            sy.wait_ge(sA.h, A0 + (ld - 2) * 32)
                        sy.wait_ge(sqst[0].h, sqst[0].n)
                        sy.wait_ge(sqst[1].h, sqst[1].n)
                        qload[ld] = sqb[ld % 3].inc(sy.dma_start(
                            out=q_buf[ld % 3][:, :], in_=q_sp[ld * 128:(ld + 1) * 128, :]), 16)

                @block.tensor
                def _(t, kvh=kvh, hh=hh, idx=idx):
                    kv = kbuf[kvh % 2]
                    vv = vbuf[kvh % 2]
                    qv = q_buf[idx % 3]
                    if idx == 0:
                        t.wait_ge(sVE.h, P4_VE_END)
                    t.wait_ge(skv[kvh % 2].h, kvload[kvh])
                    t.wait_ge(sqb[idx % 3].h, qload[idx])

                    def emit_s1(kt):
                        e = idx * 32 + kt
                        mq = kt // 4
                        c0 = mq * 128
                        if e >= 2:
                            t.wait_ge(sEX.h, EX0 + e - 1)
                        if mq < 4:
                            t.matmul(pscore[e % 2][:, c0:512], kv[:, kt % 4, mq, :],
                                     qv[:, c0:512], start=True, stop=True)
                            sA.inc(t.matmul(pscore[e % 2][:, 512:TOK], kv[:, kt % 4, mq, :],
                                            qv[:, 512:TOK], start=True, stop=True))
                        else:
                            sA.inc(t.matmul(pscore[e % 2][:, c0:TOK], kv[:, kt % 4, mq, :],
                                            qv[:, c0:TOK], start=True, stop=True))

                    def emit_s2(kt):
                        e = idx * 32 + kt
                        mq = kt // 4
                        c0 = mq * 128
                        t.wait_ge(sMK.h, MK0 + e + 1)
                        if kt == 0 and idx > 0:
                            t.wait_ge(sVE.h, VE5 + idx * 4)
                        st_, sp_ = (kt == 0), (kt == 31)
                        if mq < 4:
                            t.matmul(pout[:, c0:512], vv[:, kt % 4, mq, :],
                                     ptil[e % PT][:, c0:512], start=st_, stop=sp_)
                        t.matmul(pout[:, max(c0, 512):TOK], vv[:, kt % 4, mq, :],
                                 ptil[e % PT][:, max(c0, 512):TOK], start=st_, stop=sp_)
                        if mq < 4:
                            t.matmul(prow[0:1, c0:512], onesc_sb[:, 0:1],
                                     ptil[e % PT][:, c0:512], start=st_, stop=sp_)
                        sB.inc(t.matmul(prow[0:1, max(c0, 512):TOK], onesc_sb[:, 0:1],
                                        ptil[e % PT][:, max(c0, 512):TOK], start=st_, stop=sp_))

                    for kt in range(32):
                        emit_s1(kt)
                        if kt == 8 and idx > 0:
                            emit_bcast(t, idx - 1)
                        if kt >= LAG:
                            emit_s2(kt - LAG)
                    for kt in range(32 - LAG, 32):
                        emit_s2(kt)
                    if idx == NH - 1:
                        emit_bcast(t, idx)

                @block.vector
                def _(v, idx=idx, h=h):
                    for kt in range(32):
                        e = idx * 32 + kt
                        mq = kt // 4
                        v.wait_ge(sEX.h, EX0 + e + 1)
                        sMK.inc(v.tensor_tensor(ptil[e % PT][:, mq * 128:(mq + 1) * 128],
                                                ptil[e % PT][:, mq * 128:(mq + 1) * 128],
                                                hostm_sb[:, kt % 4, :], ALU.mult))
                    v.wait_ge(sB.h, B0 + idx * 32 + 32)
                    sVE.inc(v.reciprocal(inv_f[0:1, :], prow[0:1, :]))
                    v.wait_ge(sVE.h, sVE.n)
                    sVE.inc(v.tensor_copy(inv_bf[0:1, :], inv_f[0:1, :]))
                    v.wait_ge(sPEb.h, PEB0 + idx + 1)
                    sVE.inc(v.tensor_copy(bcast_sb[:, :], prow[:, :]))
                    v.wait_ge(sVE.h, sVE.n)
                    sVE.inc(v.tensor_tensor(ao[:, h, :], pout[:, :], bcast_sb[:, :], ALU.mult))
                assert sVE.n == VE5 + (idx + 1) * 4, (sVE.n, VE5, idx)
        @block.scalar
        def _(s):
            for idx2 in range(NH):
                for kt in range(32):
                    e = idx2 * 32 + kt
                    mq = kt // 4
                    s.wait_ge(sA.h, A0 + e + 1)
                    if e >= PT:
                        s.wait_ge(sB.h, B0 + e - PT + 1)
                    sEX.inc(s.activation(ptil[e % PT][:, mq * 128:TOK],
                                         pscore[e % 2][:, mq * 128:TOK], AF.Exp,
                                         bias=ebias_sb[:, 0:1], scale=SCALE))

        AO_DONE = sVE.n
        assert sPEb.n == PEB0 + NH
        if _DEBUG[0]:
            @block.sync
            def _(sy):
                sy.wait_ge(sVE.h, AO_DONE)
                sdbg.inc(sy.dma_start(out=dbg_ao, in_=ao[:, :, :]), 16)
        assert sEX.n == EX0 + NH * 32 and sA.n == A0 + NH * 32 and sMK.n == MK0 + NH * 32
        s_p5.close()

        s_p67 = ExitStack()
        sg = s_p67.enter_context(nc.sbuf_tensor("sg", [128, 16, TOK], F32))
        s_p6 = ExitStack()
        g_buf = [s_p6.enter_context(nc.sbuf_tensor(f"gb{i}", [128, TOK], BF16)) for i in range(2)]
        ps6 = [s_p6.enter_context(nc.psum_tensor(f"ps6{i}", [128, TOK], F32)) for i in range(2)]

        gload = [0] * 16
        for dchunk in range(16):
            slot = dchunk % 4
            par = dchunk % 2

            @block.gpsimd
            def _(gp, dchunk=dchunk):
                def wo_load(un, sl):
                    wait_free(gp, f"wslot{sl}")
                    wld[("wo", un)] = sw[sl].inc(gp.dma_start(
                        out=wpool[:, sl, :].rearrange("p (k c) -> p k c", c=128),
                        in_=wo_blk[un, :, :, :]), 16)
                if dchunk == 0:
                    for j in range(3):
                        wo_load(j, j)
                if dchunk + 3 < 16:
                    wo_load(dchunk + 3, (dchunk + 3) % 4)

            @block.sync
            def _(sy, dchunk=dchunk, par=par):
                if dchunk < 2:
                    sy.wait_ge(sVE.h, AO_DONE)
                wait_free(sy, f"gb{par}")
                gsem, gval = g_store_mark[dchunk]
                sy.wait_ge(gsem.h, gval)
                gload[dchunk] = sgb[par].inc(sy.dma_start(
                    out=g_buf[par][:, :], in_=g_sp[dchunk * 128:(dchunk + 1) * 128, :]), 16)

            @block.tensor
            def _(t, dchunk=dchunk, slot=slot, par=par):
                wv = wpool[:, slot, :].rearrange("p (k c) -> p k c", c=128)
                t.wait_ge(sw[slot].h, wld[("wo", dchunk)])
                if dchunk == 0:
                    t.wait_ge(sVE.h, AO_DONE)
                wait_free(t, f"ps6{par}")
                ins = None
                for hx in range(NH):
                    t.matmul(ps6[par][:, 0:512], wv[:, hx, :], ao[:, hx, 0:512],
                             start=(hx == 0), stop=(hx == NH - 1))
                    ins = t.matmul(ps6[par][:, 512:TOK], wv[:, hx, :], ao[:, hx, 512:TOK],
                                   start=(hx == 0), stop=(hx == NH - 1))
                m = sPE.inc(ins)
                set_free(f"wslot{slot}", sPE, m)
            s_done = sPE.n

            @block.vector
            def _(v, dchunk=dchunk, par=par, s_done=s_done):
                v.wait_ge(sPE.h, s_done)
                v.wait_ge(sgb[par].h, gload[dchunk])
                m = sVE.inc(v.tensor_tensor(sg[:, dchunk, :], ps6[par][:, :], g_buf[par][:, :],
                                            ALU.mult))
                set_free(f"ps6{par}", sVE, m)
                set_free(f"gb{par}", sVE, m)
        SG_DONE = sVE.n
        if _DEBUG[0]:
            @block.sync
            def _(sy):
                sy.wait_ge(sVE.h, SG_DONE)
                sdbg.inc(sy.dma_start(out=dbg_sg, in_=sg[:, :, :]), 16)
        s_p6.close()
        s_ao.close()

        s_p7 = ExitStack()
        u7 = [s_p7.enter_context(nc.sbuf_tensor(f"u7{i}", [128, D], F32)) for i in range(2)]
        ost = [s_p7.enter_context(nc.sbuf_tensor(f"ost{i}", [128, D], F32)) for i in range(2)]
        pp7 = [s_p7.enter_context(nc.psum_tensor(f"pp7{i}", [128, 512], F32)) for i in range(2)]

        u7load = [0] * 8

        @block.sync
        def _(sy):
            sy.wait_ge(sVE.h, SG_DONE)
            for t in range(2):
                u7load[t] = su[t].inc(sy.dma_start(out=u7[t][:, :], in_=u_c[t * 128:(t + 1) * 128, :]), 16)

        for tt in range(8):
            par = tt % 2
            for grp in range(4):
                gpar = grp % 2

                @block.tensor
                def _(t, tt=tt, grp=grp, gpar=gpar):
                    if tt == 0 and grp == 0:
                        t.wait_ge(sVE.h, SG_DONE)
                    wait_free(t, f"pp7{gpar}")
                    ins = None
                    for j in range(4):
                        dchunk = grp * 4 + j
                        ins = t.transpose(pp7[gpar][:, j * 128:(j + 1) * 128],
                                          sg[:, dchunk, tt * 128:(tt + 1) * 128], idf_sb[:, :])
                    sPE.inc(ins)
                tr_done = sPE.n

                @block.vector
                def _(v, tt=tt, grp=grp, gpar=gpar, par=par, tr_done=tr_done):
                    v.wait_ge(sPE.h, tr_done)
                    v.wait_ge(su[tt % 2].h, u7load[tt])
                    if grp == 0:
                        wait_free(v, f"ost{par}")
                    m = sVE.inc(v.tensor_tensor(ost[par][:, grp * 512:(grp + 1) * 512],
                                                pp7[gpar][:, :],
                                                u7[par][:, grp * 512:(grp + 1) * 512], ALU.add))
                    set_free(f"pp7{gpar}", sVE, m)
                    if grp == 3:
                        set_free(f"u7b{par}", sVE, m)
            row_done = sVE.n

            @block.sync
            def _(sy, tt=tt, par=par, row_done=row_done):
                sy.wait_ge(sVE.h, row_done)
                m = sost[par].inc(sy.dma_start(out=out_d[tt * 128:(tt + 1) * 128, :], in_=ost[par][:, :]), 16)
                set_free(f"ost{par}", sost[par], m)
                if tt + 2 < 8:
                    wait_free(sy, f"u7b{par}")
                    u7load[tt + 2] = su[par].inc(
                        sy.dma_start(out=u7[par][:, :], in_=u_c[(tt + 2) * 128:(tt + 3) * 128, :]), 16)

        @block.sync
        def _(sy):
            for ctr in [sost[0], sost[1], sqst[0], sqst[1], sgst[0], sgst[1],
                        svst[0], svst[1], sks, sdbg]:
                if ctr.n:
                    sy.wait_ge(ctr.h, ctr.n)

        s_p7.close()
        s_p67.close()

    return nc


_NC_CACHE = [None]


def _host_prep(u, W_in, W_out, w_gate, w_up, w_down):
    f32 = np.float32
    bf16 = ml_dtypes.bfloat16

    def blk(wT):
        K, C = wT.shape
        a = np.ascontiguousarray(wT).reshape(K // 128, 128, C // 128, 128)
        return np.ascontiguousarray(a.transpose(2, 1, 0, 3))

    ip = np.arange(IF)
    orig = (ip % 2048) * 4 + ip // 2048

    wi_blk = blk(np.ascontiguousarray(W_in[4096:10240].T))
    wo_blk = blk(np.ascontiguousarray(W_out.T))
    wg_blk = blk(np.ascontiguousarray(w_gate[:, orig]))
    wu_blk = blk(np.ascontiguousarray(w_up[:, orig]))
    wd_blk = blk(np.ascontiguousarray(w_down[orig, :]))

    R = np.zeros((128, 128), f32)
    for i in range(64):
        R[i, i + 64] = -1.0
        R[i + 64, i] = 1.0
    rott = np.ascontiguousarray(R.T)
    idf = np.eye(128, dtype=f32)
    onesc = np.ones((128, 1), bf16)
    ones1 = np.ones((1, 128), bf16)
    inv_freq = 1.0 / (10000.0 ** (np.arange(0, HD, 2, dtype=f32) / HD))

    per_core = []
    for c in range(8):
        bc, cz = c // 4, c % 4
        rows = np.concatenate([np.arange((4 * m + cz) * 128, (4 * m + cz) * 128 + 128)
                               for m in range(8)])
        u_cc = np.ascontiguousarray(u[bc][rows])
        pos = rows.astype(f32)
        freqs = pos[None, :] * inv_freq[:, None]
        cos_t = np.ascontiguousarray(np.concatenate([np.cos(freqs)] * 2, 0).astype(f32))
        sin_t = np.ascontiguousarray(np.concatenate([np.sin(freqs)] * 2, 0).astype(f32))
        tri = (np.arange(128)[:, None] <= np.arange(128)[None, :]).astype(f32)
        m4 = np.zeros((128, 4, 128), f32)
        for r in range(4):
            if r < cz:
                m4[:, r, :] = 1.0
            elif r == cz:
                m4[:, r, :] = tri
        per_core.append(dict(
            u_c=u_cc, wi_blk=wi_blk, wg_blk=wg_blk, wu_blk=wu_blk, wd_blk=wd_blk,
            wo_blk=wo_blk, cos_t=cos_t, sin_t=sin_t,
            hostm=np.ascontiguousarray(m4.astype(bf16)), rott=rott, idf=idf,
            onesc=onesc, ones1=ones1))
    return per_core


def kernel(L0_attn_u, L0_attn_W_in, L0_attn_W_out, L0_ffn_w_gate, L0_ffn_w_up, L0_ffn_w_down):
    in_maps = _host_prep(np.asarray(L0_attn_u, np.float32), np.asarray(L0_attn_W_in, np.float32),
                         np.asarray(L0_attn_W_out, np.float32), np.asarray(L0_ffn_w_gate, np.float32),
                         np.asarray(L0_ffn_w_up, np.float32), np.asarray(L0_ffn_w_down, np.float32))
    if _NC_CACHE[0] is None:
        _NC_CACHE[0] = build_program()
    res = run_bass_kernel_spmd(_NC_CACHE[0], in_maps, core_ids=list(range(8)))
    out = np.zeros((2, 4096, 2048), np.float32)
    for c in range(8):
        bc, cz = c // 4, c % 4
        rows = np.concatenate([np.arange((4 * m + cz) * 128, (4 * m + cz) * 128 + 128)
                               for m in range(8)])
        out[bc][rows] = np.asarray(res.results[c]["out"], np.float32)
    return out


if __name__ == "__main__":
    nc = build_program()
    try:
        print("instructions:", nc.m.n_instructions())
    except Exception:
        pass
    print("built ok")
